# version 12
# baseline (speedup 1.0000x reference)
import numpy as np
import concourse.bass as bass
import concourse.mybir as mybir
from concourse.bass_utils import run_bass_kernel_spmd

F32 = mybir.dt.float32
BF16 = mybir.dt.bfloat16
AF = mybir.ActivationFunctionType
ALU = mybir.AluOpType
AXX = mybir.AxisListType.X

B, T, C, D3, R, I = 4, 4096, 1024, 3072, 64, 2816
EPS = 1e-5
TOK = 4096
NQ = 2048
NCORE = 8
SHF = (I * C) // NCORE // 128
SHQ = SHF // 4
NEGBIG = -10000.0
M2 = [0.0625, 0.5625, 1.5625, 3.0625, 6.25, 12.25, 25.0]
WTS = [0.5, 0.5, 0.5, 0.5, 1.0, 1.0, 2.0]
INV1024 = 1.0 / 1024.0
INV16384 = 1.0 / 16384.0
SMSCALE = 0.03125


class Sem:
    __slots__ = ("h", "n")

    def __init__(self, h):
        self.h = h
        self.n = 0

    def inc(self, ins, k=1):
        ins.then_inc(self.h, k)
        self.n += k
        return self.n


class Gate:

    def __init__(self, sem, nslots):
        self.sem = sem
        self.v = [0] * nslots

    def wait(self, eng, slot):
        if self.v[slot] > 0:
            eng.wait_ge(self.sem.h, self.v[slot])

    def release(self, slot):
        self.v[slot] = self.sem.n


def emit_quant(blk, sv, se, src, osrc, u, qg, t0, sgn, bm, sc, rs, out, oout, n):
    nb = n // 16

    @blk.vector
    def _(v):
        i = v.tensor_reduce(bm[:, 0:nb, None],
                            src.ap()[:, osrc:osrc + n].rearrange("p (a b) -> p a b", b=16),
                            axis=AXX, op=ALU.max, apply_absolute_value=True)
        sv.inc(i)
        v.wait_ge(sv.h, sv.n)
        i = v.tensor_scalar(sc[:, 0:nb], bm[:, 0:nb], 1.0 / 6.0, 1e-12,
                            op0=ALU.mult, op1=ALU.max)
        sv.inc(i)
        v.wait_ge(sv.h, sv.n)
        i = v.reciprocal(rs[:, 0:nb], sc[:, 0:nb])
        sv.inc(i)
        v.wait_ge(sv.h, sv.n)
        i = v.tensor_tensor(u.ap()[:, 0:n].rearrange("p (a b) -> p a b", b=16),
                            src.ap()[:, osrc:osrc + n].rearrange("p (a b) -> p a b", b=16),
                            rs[:, 0:nb, None].broadcast_to((128, nb, 16)),
                            op=ALU.mult)
        sv.inc(i)

    @blk.scalar
    def _(s):
        s.wait_ge(sv.h, sv.n)
        i = s.sign(sgn[:, 0:n], u[:, 0:n])
        se.inc(i)
        s.wait_ge(se.h, se.n)
        i = s.square(u[:, 0:n], u[:, 0:n])
        se.inc(i)

    @blk.vector
    def _(v):
        v.wait_ge(se.h, se.n)
        v.tensor_scalar(qg[:, 0:n], u[:, 0:n], M2[0], WTS[0],
                        op0=ALU.is_gt, op1=ALU.mult)
        for j in range(1, 7):
            v.tensor_scalar(t0[:, 0:n], u[:, 0:n], M2[j], WTS[j],
                            op0=ALU.is_gt, op1=ALU.mult)
            v.tensor_tensor(qg[:, 0:n], qg[:, 0:n], t0[:, 0:n], op=ALU.add)
        v.tensor_tensor(qg[:, 0:n], qg[:, 0:n], sgn[:, 0:n], op=ALU.mult)
        i = v.tensor_tensor(out.ap()[:, oout:oout + n].rearrange("p (a b) -> p a b", b=16),
                            qg.ap()[:, 0:n].rearrange("p (a b) -> p a b", b=16),
                            sc[:, 0:nb, None].broadcast_to((128, nb, 16)),
                            op=ALU.mult)
        sv.inc(i)


def emit_quant_sp(blk, sv, se, sg, gprev, src, osrc, u, qg, t0, sgn, bm, sc, rs,
                  out, oout, n, ncut):
    nb = n // 16
    nbc = ncut // 16

    @blk.vector
    def _(v):
        i = v.tensor_reduce(bm[:, 0:nb, None],
                            src.ap()[:, osrc:osrc + n].rearrange("p (a b) -> p a b", b=16),
                            axis=AXX, op=ALU.max, apply_absolute_value=True)
        sv.inc(i)
        v.wait_ge(sv.h, sv.n)
        i = v.tensor_scalar(sc[:, 0:nb], bm[:, 0:nb], 1.0 / 6.0, 1e-12,
                            op0=ALU.mult, op1=ALU.max)
        sv.inc(i)
        v.wait_ge(sv.h, sv.n)
        i = v.reciprocal(rs[:, 0:nb], sc[:, 0:nb])
        sv.inc(i)
        v.wait_ge(sv.h, sv.n)
        if gprev[0]:
            v.wait_ge(sg.h, gprev[0])
        i = v.tensor_tensor(u.ap()[:, 0:n].rearrange("p (a b) -> p a b", b=16),
                            src.ap()[:, osrc:osrc + n].rearrange("p (a b) -> p a b", b=16),
                            rs[:, 0:nb, None].broadcast_to((128, nb, 16)),
                            op=ALU.mult)
        sv.inc(i)

    @blk.scalar
    def _(s):
        s.wait_ge(sv.h, sv.n)
        i = s.sign(sgn[:, 0:n], u[:, 0:n])
        se.inc(i)
        s.wait_ge(se.h, se.n)
        i = s.square(u[:, 0:n], u[:, 0:n])
        se.inc(i)

    def cmp_chain(eng, a, b, sem):
        w = b - a
        wb = w // 16
        eng.tensor_scalar(qg[:, a:b], u[:, a:b], M2[0], WTS[0],
                          op0=ALU.is_gt, op1=ALU.mult)
        for j in range(1, 7):
            eng.tensor_scalar(t0[:, a:b], u[:, a:b], M2[j], WTS[j],
                              op0=ALU.is_gt, op1=ALU.mult)
            eng.tensor_tensor(qg[:, a:b], qg[:, a:b], t0[:, a:b], op=ALU.add)
        eng.tensor_tensor(qg[:, a:b], qg[:, a:b], sgn[:, a:b], op=ALU.mult)
        i = eng.tensor_tensor(
            out.ap()[:, oout + a:oout + b].rearrange("p (c d) -> p c d", d=16),
            qg.ap()[:, a:b].rearrange("p (c d) -> p c d", d=16),
            sc[:, a // 16:a // 16 + wb, None].broadcast_to((128, wb, 16)),
            op=ALU.mult)
        sem.inc(i)

    @blk.vector
    def _(v):
        v.wait_ge(se.h, se.n)
        cmp_chain(v, 0, ncut, sv)

    @blk.gpsimd
    def _(g):
        g.wait_ge(se.h, se.n)
        cmp_chain(g, ncut, n, sg)
    gprev[0] = sg.n


def emit_ln(blk, sv, se, vres, outt, lnt, sl, sqdump):
    a, b_, c_, d_ = sl * 4, sl * 4 + 1, sl * 4 + 2, sl * 4 + 3
    o = sl * 1024

    @blk.vector
    def _(v):
        i = v.tensor_reduce(lnt[:, a:a + 1], vres[:, o:o + 1024], axis=AXX, op=ALU.add)
        sv.inc(i)

    @blk.scalar
    def _(s):
        s.wait_ge(sv.h, sv.n)
        i = s.activation(sqdump[:, :], vres[:, o:o + 1024], AF.Square,
                         accum_out=lnt[:, b_:b_ + 1])
        se.inc(i)

    @blk.vector
    def _(v):
        v.wait_ge(se.h, se.n)
        i = v.tensor_scalar(lnt[:, a:a + 1], lnt[:, a:a + 1], INV1024, None, op0=ALU.mult)
        sv.inc(i)
        v.wait_ge(sv.h, sv.n)
        i = v.tensor_scalar(lnt[:, b_:b_ + 1], lnt[:, b_:b_ + 1], INV1024, None, op0=ALU.mult)
        sv.inc(i)
        v.wait_ge(sv.h, sv.n)
        i = v.tensor_tensor(lnt[:, c_:c_ + 1], lnt[:, a:a + 1], lnt[:, a:a + 1], op=ALU.mult)
        sv.inc(i)
        v.wait_ge(sv.h, sv.n)
        i = v.tensor_tensor(lnt[:, b_:b_ + 1], lnt[:, b_:b_ + 1], lnt[:, c_:c_ + 1],
                            op=ALU.subtract)
        sv.inc(i)
        v.wait_ge(sv.h, sv.n)
        i = v.tensor_scalar(lnt[:, b_:b_ + 1], lnt[:, b_:b_ + 1], EPS, None, op0=ALU.add)
        sv.inc(i)

    @blk.scalar
    def _(s):
        s.wait_ge(sv.h, sv.n)
        i = s.activation(lnt[:, c_:c_ + 1], lnt[:, b_:b_ + 1], AF.Sqrt)
        se.inc(i)

    @blk.vector
    def _(v):
        v.wait_ge(se.h, se.n)
        i = v.reciprocal(lnt[:, d_:d_ + 1], lnt[:, c_:c_ + 1])
        sv.inc(i)
        v.wait_ge(sv.h, sv.n)
        i = v.tensor_tensor(lnt[:, c_:c_ + 1], lnt[:, a:a + 1], lnt[:, d_:d_ + 1],
                            op=ALU.mult)
        sv.inc(i)
        v.wait_ge(sv.h, sv.n)
        i = v.tensor_scalar(lnt[:, c_:c_ + 1], lnt[:, c_:c_ + 1], -1.0, None, op0=ALU.mult)
        sv.inc(i)
        v.wait_ge(sv.h, sv.n)
        i = v.tensor_scalar(outt[:, o:o + 1024], vres[:, o:o + 1024],
                            lnt[:, d_:d_ + 1], lnt[:, c_:c_ + 1],
                            op0=ALU.mult, op1=ALU.add)
        sv.inc(i)


def build_nc():
    import contextlib
    nc = bass.Bass("TRN2", target_bir_lowering=False)

    xin = nc.dram_tensor("x", [TOK, C], F32, kind="ExternalInput")
    w_attn = nc.dram_tensor("w_attn", [D3, C], F32, kind="ExternalInput")
    w_proj = nc.dram_tensor("w_proj", [C, C], F32, kind="ExternalInput")
    laa = nc.dram_tensor("laa", [C, R], F32, kind="ExternalInput")
    lab = nc.dram_tensor("lab", [R, D3], F32, kind="ExternalInput")
    lpa = nc.dram_tensor("lpa", [C, R], F32, kind="ExternalInput")
    lpb = nc.dram_tensor("lpb", [R, C], F32, kind="ExternalInput")
    gam2d = nc.dram_tensor("gam2d", [128, 24], F32, kind="ExternalInput")
    bet2d = nc.dram_tensor("bet2d", [128, 24], F32, kind="ExternalInput")
    wg_sh = nc.dram_tensor("wg_sh", [128, SHF], F32, kind="ExternalInput")
    wu_sh = nc.dram_tensor("wu_sh", [128, SHF], F32, kind="ExternalInput")
    wd_sh = nc.dram_tensor("wd_sh", [128, SHF], F32, kind="ExternalInput")
    kidx_in = nc.dram_tensor("kidx", [128, TOK], F32, kind="ExternalInput")
    qpos_in = nc.dram_tensor("qpos", [128, 16], F32, kind="ExternalInput")
    ones_in = nc.dram_tensor("ones", [128, 1], F32, kind="ExternalInput")
    yout = nc.dram_tensor("y", [NQ, C], F32, kind="ExternalOutput")

    xp_b = nc.dram_tensor("xp_b", [TOK, C], BF16)
    wab = nc.dram_tensor("wab", [D3, C], BF16)
    wpb = nc.dram_tensor("wpb", [C, C], BF16)
    laa_b = nc.dram_tensor("laa_b", [C, 128], BF16)
    lpa_b = nc.dram_tensor("lpa_b", [C, 128], BF16)
    lab_b = nc.dram_tensor("lab_b", [R, D3], BF16)
    lpb_b = nc.dram_tensor("lpb_b", [R, C], BF16)
    k_dram = nc.dram_tensor("k_dram", [C, TOK], BF16)
    v_dram = nc.dram_tensor("v_dram", [TOK, C], BF16)
    p_dram = nc.dram_tensor("p_dram", [NQ, TOK], BF16)
    xq_dram = nc.dram_tensor("xq_dram", [NQ, C], BF16)
    hq_dram = nc.dram_tensor("hq_dram", [NQ, I], BF16)
    x2_dram = nc.dram_tensor("x2_dram", [NQ, C], F32)
    agi_g = nc.dram_tensor("agi_g", [128, SHF], BF16)
    agi_u = nc.dram_tensor("agi_u", [128, SHF], BF16)
    agi_d = nc.dram_tensor("agi_d", [128, SHF], BF16)
    wgq = nc.dram_tensor("wgq", [I, C], BF16)
    wuq = nc.dram_tensor("wuq", [I, C], BF16)
    wdq = nc.dram_tensor("wdq", [C, I], BF16)
    ccin = nc.dram_tensor("ccin", [128, 48], F32)
    ccout = nc.dram_tensor("ccout", [128, 48], F32)

    ctx = contextlib.ExitStack()
    with ctx:
        sg = Sem(ctx.enter_context(nc.semaphore("sg")))
        st = Sem(ctx.enter_context(nc.semaphore("st")))
        ss = Sem(ctx.enter_context(nc.semaphore("ssd")))
        sm = Sem(ctx.enter_context(nc.semaphore("sm")))
        sv = Sem(ctx.enter_context(nc.semaphore("sv")))
        se = Sem(ctx.enter_context(nc.semaphore("se")))
        scc = Sem(ctx.enter_context(nc.semaphore("scc")))
        so = Sem(ctx.enter_context(nc.semaphore("so")))

        gamma = ctx.enter_context(nc.sbuf_tensor("gamma_sb", [128, 24], F32))
        beta = ctx.enter_context(nc.sbuf_tensor("beta_sb", [128, 24], F32))
        stA = ctx.enter_context(nc.sbuf_tensor("stA", [128, 24], F32))
        stB = ctx.enter_context(nc.sbuf_tensor("stB", [128, 24], F32))
        aqk = ctx.enter_context(nc.sbuf_tensor("aqk", [128, 8], F32))
        bqk = ctx.enter_context(nc.sbuf_tensor("bqk", [128, 8], F32))
        qpos = ctx.enter_context(nc.sbuf_tensor("qpos_sb", [128, 16], F32))
        onec = ctx.enter_context(nc.sbuf_tensor("onec", [128, 1], BF16))
        smx = ctx.enter_context(nc.sbuf_tensor("smx", [128, 64], F32))
        sqdump = ctx.enter_context(nc.sbuf_tensor("sqdump", [128, 1024], BF16))

        blk = ctx.enter_context(nc.Block())

        @blk.gpsimd
        def _(g):
            g.dma_start(out=wab.ap(), in_=w_attn.ap()).then_inc(sg.h, 16)
            g.dma_start(out=laa_b.ap()[:, 0:R], in_=laa.ap()).then_inc(sg.h, 16)
            g.dma_start(out=lab_b.ap(), in_=lab.ap()).then_inc(sg.h, 16)
        sg.n += 3 * 16
        WAB_C = sg.n

        @blk.gpsimd
        def _(g):
            g.dma_start(out=xp_b.ap(), in_=xin.ap()).then_inc(sg.h, 16)
        sg.n += 16
        XPB_C = sg.n

        @blk.gpsimd
        def _(g):
            g.dma_start(out=wpb.ap(), in_=w_proj.ap()).then_inc(sg.h, 16)
            g.dma_start(out=lpa_b.ap()[:, 0:R], in_=lpa.ap()).then_inc(sg.h, 16)
            g.dma_start(out=lpb_b.ap(), in_=lpb.ap()).then_inc(sg.h, 16)
            g.dma_start(out=onec[:, :], in_=ones_in.ap()).then_inc(sg.h, 16)
        sg.n += 4 * 16
        CASTS = sg.n

        @blk.scalar
        def _(s):
            s.dma_start(out=gamma[:, :], in_=gam2d.ap()).then_inc(ss.h, 16)
            s.dma_start(out=beta[:, :], in_=bet2d.ap()).then_inc(ss.h, 16)
            s.dma_start(out=qpos[:, :], in_=qpos_in.ap()).then_inc(ss.h, 16)
        ss.n += 3 * 16

        qstack = ctx.enter_context(contextlib.ExitStack())
        qraw = qstack.enter_context(nc.sbuf_tensor("qraw", [128, 8 * NQ], BF16))

        pA = qstack.enter_context(contextlib.ExitStack())
        xfm = pA.enter_context(nc.sbuf_tensor("xfm", [128, 8 * TOK], BF16))
        wmt = pA.enter_context(nc.sbuf_tensor("wmt", [128, 8 * D3], BF16))
        qsh = pA.enter_context(nc.sbuf_tensor("qsh", [128, SHQ], F32))
        qu = pA.enter_context(nc.sbuf_tensor("qu", [128, SHQ], F32))
        qg_ = pA.enter_context(nc.sbuf_tensor("qg_", [128, SHQ], BF16))
        qt_ = pA.enter_context(nc.sbuf_tensor("qt_", [128, SHQ], BF16))
        qs_ = pA.enter_context(nc.sbuf_tensor("qs_", [128, SHQ], BF16))
        qq_ = pA.enter_context(nc.sbuf_tensor("qq_", [128, SHQ], BF16))
        qbm = pA.enter_context(nc.sbuf_tensor("qbm", [128, SHQ // 16], F32))
        qsc = pA.enter_context(nc.sbuf_tensor("qsc", [128, SHQ // 16], F32))
        qrs = pA.enter_context(nc.sbuf_tensor("qrs", [128, SHQ // 16], F32))

        SHARD_SRCS = ((wg_sh, agi_g), (wu_sh, agi_u), (wd_sh, agi_d))

        def emit_shard_chunk(i):
            src_d, agi = SHARD_SRCS[i // 4]
            ch = i % 4

            @blk.scalar
            def _(s, src_d=src_d, ch=ch):
                s.wait_ge(sv.h, sv.n)
                s.dma_start(out=qsh[:, :],
                            in_=src_d.ap()[:, ch * SHQ:(ch + 1) * SHQ]).then_inc(ss.h, 16)
            ss.n += 16

            @blk.vector
            def _(v):
                v.wait_ge(ss.h, ss.n)

            @blk.scalar
            def _(s):
                s.wait_ge(ss.h, ss.n)

            emit_quant(blk, sv, se, qsh, 0, qu, qg_, qt_, qs_, qbm, qsc, qrs, qq_, 0, SHQ)

            @blk.scalar
            def _(s, agi=agi, ch=ch):
                s.wait_ge(sv.h, sv.n)
                s.dma_start(out=agi.ap()[:, ch * SHQ:(ch + 1) * SHQ],
                            in_=qq_[:, :]).then_inc(ss.h, 16)
            ss.n += 16

        aat = pA.enter_context(nc.sbuf_tensor("aat", [128, C], BF16))
        labs = pA.enter_context(nc.sbuf_tensor("labs", [64, D3], BF16))

        @blk.sync
        def _(sy):
            sy.wait_ge(sg.h, WAB_C)
            for c in range(8):
                sy.dma_start(wmt[:, c * D3:(c + 1) * D3],
                             wab.ap()[0:D3, c * 128:(c + 1) * 128],
                             transpose=True).then_inc(st.h, 16)
            sy.dma_start(aat[:, :], laa_b.ap()[0:C, 0:128], transpose=True).then_inc(st.h, 16)
            sy.dma_start(labs[:, :], lab_b.ap()).then_inc(st.h, 16)
            sy.wait_ge(sg.h, XPB_C)
            for c in range(8):
                sy.dma_start(xfm[:, c * TOK:(c + 1) * TOK],
                             xp_b.ap()[0:TOK, c * 128:(c + 1) * 128],
                             transpose=True).then_inc(st.h, 16)
        st.n += 18 * 16
        WLOADS = st.n

        with (nc.psum_tensor("psl0", [128, 512], F32) as psl0,
              nc.psum_tensor("psl1", [128, 512], F32) as psl1):
            psl = [psl0, psl1]
            lgate = Gate(sv, 2)
            for idx in range(48):
                jc, hf = idx // 2, idx % 2
                slot = idx % 2
                pb = psl[slot]

                @blk.tensor
                def _(t, pb=pb, idx=idx, jc=jc, hf=hf, slot=slot):
                    if idx == 0:
                        t.wait_ge(st.h, WLOADS - 8 * 16)
                    lgate.wait(t, slot)
                    ii = t.matmul(pb[:, :], labs[0:64, jc * 128:(jc + 1) * 128],
                                  aat[0:64, hf * 512:(hf + 1) * 512],
                                  start=True, stop=True)
                    sm.inc(ii)

                @blk.vector
                def _(v, pb=pb, jc=jc, hf=hf, slot=slot):
                    v.wait_ge(sm.h, sm.n)
                    s3, cch = jc // 8, jc % 8
                    o0 = cch * D3 + s3 + 3 * 512 * hf
                    tgt = wmt.ap()[:, o0:o0 + 3 * 511 + 1:3]
                    ii = v.tensor_tensor(tgt, tgt, pb[:, :], op=ALU.add)
                    sv.inc(ii)
                    lgate.release(slot)
        LORA_V = sv.n

        kvst = pA.enter_context(nc.sbuf_tensor("kvst", [128, 4 * 512], BF16))
        vsq = pA.enter_context(nc.sbuf_tensor("vsq", [128, 4 * 512], BF16))
        qp_s = pA.enter_context(nc.sbuf_tensor("qp_s", [128, 32], F32))
        qp_q = pA.enter_context(nc.sbuf_tensor("qp_q", [128, 32], F32))
        kp_s = pA.enter_context(nc.sbuf_tensor("kp_s", [128, 32], F32))
        kp_q = pA.enter_context(nc.sbuf_tensor("kp_q", [128, 32], F32))
        stat48 = pA.enter_context(nc.sbuf_tensor("stat48", [128, 48], F32))

        with (nc.psum_tensor("pr0", [128, 512], F32) as pr0,
              nc.psum_tensor("pr1", [128, 512], F32) as pr1,
              nc.psum_tensor("pr2", [128, 512], F32) as pr2,
              nc.psum_tensor("pr3", [128, 512], F32) as pr3,
              nc.psum_tensor("psv", [128, 16], F32) as psv):
            prr = [pr0, pr1, pr2, pr3]
            bgate = Gate(se, 4)
            vgate = Gate(sv, 4)
            kstg = Gate(st, 4)

            for it in range(64):
                tile, dg = it // 2, it % 2
                bank = it % 4
                stg = it % 4
                pb = prr[bank]

                @blk.tensor
                def _(t, pb=pb, tile=tile, dg=dg, bank=bank, it=it):
                    if it == 0:
                        t.wait_ge(sv.h, LORA_V)
                    vgate.wait(t, bank)
                    for cch in range(8):
                        ii = t.matmul(pb[:, :],
                                      xfm[:, cch * TOK + tile * 128:cch * TOK + (tile + 1) * 128],
                                      wmt[:, cch * D3 + 2048 + dg * 512:cch * D3 + 2048 + (dg + 1) * 512],
                                      start=(cch == 0), stop=(cch == 7))
                    sm.inc(ii)

                @blk.vector
                def _(v, pb=pb, stg=stg, bank=bank):
                    v.wait_ge(sm.h, sm.n)
                    kstg.wait(v, stg)
                    ii = v.tensor_copy(kvst[:, stg * 512:(stg + 1) * 512], pb[:, :])
                    sv.inc(ii)
                    vgate.release(bank)

                @blk.scalar
                def _(s, stg=stg, tile=tile, dg=dg):
                    s.wait_ge(sv.h, sv.n)
                    s.dma_start(out=v_dram.ap()[tile * 128:(tile + 1) * 128, dg * 512:(dg + 1) * 512],
                                in_=kvst[:, stg * 512:(stg + 1) * 512]).then_inc(st.h, 16)
                st.n += 16
                kstg.release(stg)
                if it % 4 == 0 and it // 4 < 12:
                    emit_shard_chunk(it // 4)

            AGI_STORED = ss.n

            @blk.gpsimd
            def _(g):
                g.wait_ge(ss.h, AGI_STORED)
                for i_, o_ in ((agi_g, wgq), (agi_u, wuq), (agi_d, wdq)):
                    ii = g.collective_compute("AllGather", ALU.bypass,
                                              replica_groups=[list(range(NCORE))],
                                              ins=[i_.ap()], outs=[o_.ap()])
                    scc.inc(ii)
            AG_DONE = scc.n

            for it in range(64):
                tg, d = it // 8, it % 8
                bank = it % 4
                stg = it % 4
                own = tg < 4
                pb = prr[bank]

                @blk.tensor
                def _(t, pb=pb, tg=tg, d=d, bank=bank):
                    bgate.wait(t, bank)
                    vgate.wait(t, bank)
                    for cch in range(8):
                        ii = t.matmul(pb[:, :],
                                      wmt[:, cch * D3 + 1024 + d * 128:cch * D3 + 1024 + (d + 1) * 128],
                                      xfm[:, cch * TOK + tg * 512:cch * TOK + (tg + 1) * 512],
                                      start=(cch == 0), stop=(cch == 7))
                    sm.inc(ii)

                if own:
                    @blk.scalar
                    def _(s, pb=pb, tg=tg, d=d, bank=bank, stg=stg):
                        s.wait_ge(sm.h, sm.n)
                        kstg.wait(s, stg)
                        s.activation(kvst[:, stg * 512:(stg + 1) * 512], pb[:, :], AF.Copy,
                                     accum_out=kp_s[:, d * 4 + tg:d * 4 + tg + 1])
                        ii = s.activation(sqdump[:, 0:512], pb[:, :], AF.Square,
                                          accum_out=kp_q[:, d * 4 + tg:d * 4 + tg + 1])
                        se.inc(ii)
                        bgate.release(bank)
                        s.wait_ge(se.h, se.n)
                        s.dma_start(out=k_dram.ap()[d * 128:(d + 1) * 128, tg * 512:(tg + 1) * 512],
                                    in_=kvst[:, stg * 512:(stg + 1) * 512]).then_inc(st.h, 16)
                else:
                    @blk.vector
                    def _(v, pb=pb, stg=stg, bank=bank):
                        v.wait_ge(sm.h, sm.n)
                        kstg.wait(v, stg)
                        ii = v.tensor_copy(kvst[:, stg * 512:(stg + 1) * 512], pb[:, :])
                        sv.inc(ii)
                        vgate.release(bank)

                    @blk.scalar
                    def _(s, tg=tg, d=d, stg=stg):
                        s.wait_ge(sv.h, sv.n)
                        s.dma_start(out=k_dram.ap()[d * 128:(d + 1) * 128, tg * 512:(tg + 1) * 512],
                                    in_=kvst[:, stg * 512:(stg + 1) * 512]).then_inc(st.h, 16)
                st.n += 16
                kstg.release(stg)

            VSTATS_ST = st.n
            sq_se = [0] * 32
            mm_sm = [0] * 32

            def emit_statsmm(step):
                ti, dg = step // 2, step % 2
                sl4 = step % 4

                @blk.tensor
                def _(t, ti=ti, dg=dg, sl4=sl4, step=step):
                    t.wait_ge(se.h, sq_se[step])
                    for sl in range(4):
                        t.matmul(psv[:, dg * 4 + sl:dg * 4 + sl + 1],
                                 kvst[:, sl4 * 512 + sl * 128:sl4 * 512 + (sl + 1) * 128],
                                 onec[:, 0:1],
                                 start=(ti == 0), stop=(ti == 15))
                        ii = t.matmul(psv[:, 8 + dg * 4 + sl:9 + dg * 4 + sl],
                                      vsq[:, sl4 * 512 + sl * 128:sl4 * 512 + (sl + 1) * 128],
                                      onec[:, 0:1],
                                      start=(ti == 0), stop=(ti == 15))
                    sm.inc(ii)
                mm_sm[step] = sm.n

            for it in range(32):
                tg, d = it // 8, it % 8
                bank = it % 4
                pb = prr[bank]

                @blk.tensor
                def _(t, pb=pb, tg=tg, d=d, bank=bank):
                    bgate.wait(t, bank)
                    vgate.wait(t, bank)
                    for cch in range(8):
                        ii = t.matmul(pb[:, :],
                                      wmt[:, cch * D3 + d * 128:cch * D3 + (d + 1) * 128],
                                      xfm[:, cch * TOK + tg * 512:cch * TOK + (tg + 1) * 512],
                                      start=(cch == 0), stop=(cch == 7))
                    sm.inc(ii)

                if it >= 4:
                    emit_statsmm(it - 4)

                step, sl4 = it, it % 4
                ti2, dg2 = step // 2, step % 2

                @blk.sync
                def _(sy, ti2=ti2, dg2=dg2, sl4=sl4, step=step):
                    if step < 4:
                        sy.wait_ge(st.h, VSTATS_ST)
                    else:
                        sy.wait_ge(sm.h, mm_sm[step - 4])
                    sy.dma_start(out=kvst[:, sl4 * 512:(sl4 + 1) * 512],
                                 in_=v_dram.ap()[ti2 * 128:(ti2 + 1) * 128, dg2 * 512:(dg2 + 1) * 512]
                                 ).then_inc(st.h, 16)
                st.n += 16
                ld_st = st.n

                @blk.scalar
                def _(s, pb=pb, tg=tg, d=d, bank=bank, sl4=sl4, step=step, ld_st=ld_st):
                    s.wait_ge(sm.h, sm.n)
                    s.activation(qraw[:, d * NQ + tg * 512:d * NQ + (tg + 1) * 512],
                                 pb[:, :], AF.Copy,
                                 accum_out=qp_s[:, d * 4 + tg:d * 4 + tg + 1])
                    ii = s.activation(sqdump[:, 0:512], pb[:, :], AF.Square,
                                      accum_out=qp_q[:, d * 4 + tg:d * 4 + tg + 1])
                    se.inc(ii)
                    bgate.release(bank)
                    s.wait_ge(st.h, ld_st)
                    ii = s.square(vsq[:, sl4 * 512:(sl4 + 1) * 512],
                                  kvst[:, sl4 * 512:(sl4 + 1) * 512])
                    se.inc(ii)
                    sq_se[step] = se.n

            for step in range(28, 32):
                emit_statsmm(step)

            @blk.vector
            def _(v):
                v.wait_ge(sm.h, sm.n)
                v.wait_ge(se.h, se.n)
                for d in range(8):
                    v.tensor_reduce(smx[:, d:d + 1], qp_s[:, d * 4:(d + 1) * 4], axis=AXX, op=ALU.add)
                    v.tensor_reduce(smx[:, 24 + d:25 + d], qp_q[:, d * 4:(d + 1) * 4], axis=AXX, op=ALU.add)
                    v.tensor_reduce(smx[:, 8 + d:9 + d], kp_s[:, d * 4:(d + 1) * 4], axis=AXX, op=ALU.add)
                    v.tensor_reduce(smx[:, 32 + d:33 + d], kp_q[:, d * 4:(d + 1) * 4], axis=AXX, op=ALU.add)
                v.tensor_copy(smx[:, 16:24], psv[:, 0:8])
                ii = v.tensor_copy(smx[:, 40:48], psv[:, 8:16])
                sv.inc(ii)

        @blk.scalar
        def _(s):
            s.wait_ge(sv.h, sv.n)
            s.dma_start(out=ccin.ap(), in_=smx[:, 0:48]).then_inc(ss.h, 16)
        ss.n += 16

        @blk.gpsimd
        def _(g):
            g.wait_ge(ss.h, ss.n)
            ii = g.collective_compute("AllReduce", ALU.add,
                                      replica_groups=[list(range(NCORE))],
                                      ins=[ccin.ap()], outs=[ccout.ap()])
            scc.inc(ii)

        @blk.scalar
        def _(s):
            s.wait_ge(scc.h, scc.n)
            s.dma_start(out=stat48[:, :], in_=ccout.ap()).then_inc(ss.h, 16)
        ss.n += 16

        @blk.vector
        def _(v):
            v.wait_ge(ss.h, ss.n)
            ii = v.tensor_scalar(smx[:, 0:24], stat48[:, 0:24], INV16384, None, op0=ALU.mult)
            sv.inc(ii)
            v.wait_ge(sv.h, sv.n)
            ii = v.tensor_scalar(smx[:, 24:48], stat48[:, 24:48], INV16384, None, op0=ALU.mult)
            sv.inc(ii)
            v.wait_ge(sv.h, sv.n)
            ii = v.tensor_tensor(stat48[:, 0:24], smx[:, 0:24], smx[:, 0:24], op=ALU.mult)
            sv.inc(ii)
            v.wait_ge(sv.h, sv.n)
            ii = v.tensor_tensor(stat48[:, 24:48], smx[:, 24:48], stat48[:, 0:24], op=ALU.subtract)
            sv.inc(ii)
            v.wait_ge(sv.h, sv.n)
            ii = v.tensor_scalar(stat48[:, 24:48], stat48[:, 24:48], EPS, None, op0=ALU.add)
            sv.inc(ii)

        @blk.scalar
        def _(s):
            s.wait_ge(sv.h, sv.n)
            ii = s.activation(stat48[:, 0:24], stat48[:, 24:48], AF.Sqrt)
            se.inc(ii)

        @blk.vector
        def _(v):
            v.wait_ge(se.h, se.n)
            ii = v.reciprocal(stat48[:, 24:48], stat48[:, 0:24])
            sv.inc(ii)
            v.wait_ge(sv.h, sv.n)
            ii = v.tensor_tensor(stA[:, :], gamma[:, :], stat48[:, 24:48], op=ALU.mult)
            sv.inc(ii)
            v.wait_ge(sv.h, sv.n)
            ii = v.tensor_tensor(smx[:, 0:24], smx[:, 0:24], stA[:, :], op=ALU.mult)
            sv.inc(ii)
            v.wait_ge(sv.h, sv.n)
            ii = v.tensor_tensor(stB[:, :], beta[:, :], smx[:, 0:24], op=ALU.subtract)
            sv.inc(ii)
            v.wait_ge(sv.h, sv.n)
            ii = v.tensor_tensor(aqk[:, :], stA[:, 0:8], stA[:, 8:16], op=ALU.mult)
            sv.inc(ii)
            v.wait_ge(sv.h, sv.n)
            ii = v.tensor_tensor(bqk[:, :], stB[:, 0:8], stA[:, 8:16], op=ALU.mult)
            sv.inc(ii)
            v.wait_ge(sv.h, sv.n)
            for cch in range(8):
                ii = v.tensor_scalar(qraw[:, cch * NQ:(cch + 1) * NQ],
                                     qraw[:, cch * NQ:(cch + 1) * NQ],
                                     aqk[:, cch:cch + 1], bqk[:, cch:cch + 1],
                                     op0=ALU.mult, op1=ALU.add)
                sv.inc(ii)
        QA_V = sv.n
        KV_STORED = st.n
        P1_SM = sm.n
        P1_SE = se.n

        pA.close()

        pB = qstack.enter_context(contextlib.ExitStack())
        kidxp = pB.enter_context(nc.sbuf_tensor("kidxp", [128, TOK], F32))
        kbuf = pB.enter_context(nc.sbuf_tensor("kbuf", [128, 2 * 4096], BF16))
        masks = pB.enter_context(nc.sbuf_tensor("masks", [128, 4 * TOK], BF16))
        strips = pB.enter_context(nc.sbuf_tensor("strips", [128, 4 * TOK], F32))
        pstr = pB.enter_context(nc.sbuf_tensor("pstr", [128, 2 * TOK], BF16))
        pnrm = pB.enter_context(nc.sbuf_tensor("pnrm", [128, 2 * TOK], BF16))

        @blk.sync
        def _(sy):
            sy.wait_ge(st.h, KV_STORED)
            sy.wait_ge(sm.h, P1_SM)
            sy.wait_ge(se.h, P1_SE)
            sy.wait_ge(sv.h, QA_V)
            sy.dma_start(out=kidxp[:, :], in_=kidx_in.ap()).then_inc(st.h, 16)
        st.n += 16
        KIDX_L = st.n

        with (nc.psum_tensor("sp0", [128, 512], F32) as sp0,
              nc.psum_tensor("sp1", [128, 512], F32) as sp1,
              nc.psum_tensor("sp2", [128, 512], F32) as sp2,
              nc.psum_tensor("sp3", [128, 512], F32) as sp3,
              nc.psum_tensor("sp4", [128, 512], F32) as sp4,
              nc.psum_tensor("sp5", [128, 512], F32) as sp5,
              nc.psum_tensor("sp6", [128, 512], F32) as sp6,
              nc.psum_tensor("sp7", [128, 512], F32) as sp7):
            spb = [sp0, sp1, sp2, sp3, sp4, sp5, sp6, sp7]
            sgate = Gate(sv, 8)
            kgate = Gate(sm, 2)

            for qg in range(4):
                @blk.gpsimd
                def _(g, qg=qg):
                    if qg == 0:
                        g.wait_ge(st.h, KIDX_L)
                        g.wait_ge(sv.h, QA_V)
                    else:
                        g.wait_ge(sv.h, sv.n)
                    for tt in range(4):
                        tglob = qg * 4 + tt
                        ii = g.tensor_scalar(masks[:, tt * TOK:(tt + 1) * TOK], kidxp[:, :],
                                             qpos[:, tglob:tglob + 1], NEGBIG,
                                             op0=ALU.is_gt, op1=ALU.mult)
                        sg.inc(ii)
                MGM = sg.n

                for ks in range(8):
                    kb = ks % 2

                    @blk.sync
                    def _(sy, qg=qg, ks=ks, kb=kb):
                        if qg == 0 and ks == 0:
                            sy.wait_ge(st.h, KV_STORED)
                        kgate.wait(sy, kb)
                        sy.dma_start(
                            kbuf.ap()[:, kb * 4096:(kb + 1) * 4096].rearrange("p (n k) -> p n k", k=512),
                            k_dram.ap()[:, ks * 512:(ks + 1) * 512].rearrange("(n p) k -> p n k", p=128),
                        ).then_inc(st.h, 16)
                    st.n += 16

                    for tt in range(4):
                        bank = kb * 4 + tt
                        pb = spb[bank]

                        @blk.tensor
                        def _(t, pb=pb, qg=qg, ks=ks, tt=tt, kb=kb, bank=bank):
                            if qg == 0 and ks == 0 and tt == 0:
                                t.wait_ge(sv.h, QA_V)
                            t.wait_ge(st.h, st.n)
                            sgate.wait(t, bank)
                            tglob = qg * 4 + tt
                            for cch in range(8):
                                ii = t.matmul(pb[:, :],
                                              qraw[:, cch * NQ + tglob * 128:cch * NQ + (tglob + 1) * 128],
                                              kbuf[:, kb * 4096 + cch * 512:kb * 4096 + (cch + 1) * 512],
                                              start=(cch == 0), stop=(cch == 7))
                            sm.inc(ii)
                            if tt == 3:
                                kgate.release(kb)

                        @blk.vector
                        def _(v, pb=pb, ks=ks, tt=tt, bank=bank, MGM=MGM):
                            v.wait_ge(sm.h, sm.n)
                            if ks == 0 and tt == 0:
                                v.wait_ge(sg.h, MGM)
                            ii = v.scalar_tensor_tensor(
                                strips[:, tt * TOK + ks * 512:tt * TOK + (ks + 1) * 512],
                                pb[:, :], SMSCALE,
                                masks[:, tt * TOK + ks * 512:tt * TOK + (ks + 1) * 512],
                                op0=ALU.mult, op1=ALU.add)
                            sv.inc(ii)
                            sgate.release(bank)

                for tt in range(4):
                    @blk.vector
                    def _(v, tt=tt):
                        v.wait_ge(sv.h, sv.n)
                        ii = v.tensor_reduce(smx[:, 48 + tt:49 + tt], strips[:, tt * TOK:(tt + 1) * TOK],
                                             axis=AXX, op=ALU.max, negate=True)
                        sv.inc(ii)

                    @blk.scalar
                    def _(s, tt=tt):
                        s.wait_ge(sv.h, sv.n)
                        ii = s.activation(pstr[:, (tt % 2) * TOK:(tt % 2 + 1) * TOK],
                                          strips[:, tt * TOK:(tt + 1) * TOK], AF.Exp,
                                          bias=smx[:, 48 + tt:49 + tt], scale=1.0,
                                          accum_out=smx[:, 52 + tt:53 + tt])
                        se.inc(ii)

                    @blk.vector
                    def _(v, tt=tt):
                        v.wait_ge(se.h, se.n)
                        v.wait_ge(st.h, st.n)
                        ii = v.reciprocal(smx[:, 56 + tt:57 + tt], smx[:, 52 + tt:53 + tt])
                        sv.inc(ii)
                        v.wait_ge(sv.h, sv.n)
                        ii = v.tensor_scalar(pnrm[:, (tt % 2) * TOK:(tt % 2 + 1) * TOK],
                                             pstr[:, (tt % 2) * TOK:(tt % 2 + 1) * TOK],
                                             smx[:, 56 + tt:57 + tt], None, op0=ALU.mult)
                        sv.inc(ii)

                    @blk.sync
                    def _(sy, qg=qg, tt=tt):
                        sy.wait_ge(sv.h, sv.n)
                        tglob = qg * 4 + tt
                        sy.dma_start(out=p_dram.ap()[tglob * 128:(tglob + 1) * 128, :],
                                     in_=pnrm[:, (tt % 2) * TOK:(tt % 2 + 1) * TOK]).then_inc(st.h, 16)
                    st.n += 16
        P_STORED = st.n
        P2_SM_END = sm.n
        pB.close()
        qstack.close()

        p3 = ctx.enter_context(contextlib.ExitStack())
        wpt = p3.enter_context(nc.sbuf_tensor("wpt", [128, 8 * C], BF16))
        wmpt = p3.enter_context(nc.sbuf_tensor("wmpt", [128, 8 * C], BF16))
        apt = p3.enter_context(nc.sbuf_tensor("apt", [128, C], BF16))
        lpbs = p3.enter_context(nc.sbuf_tensor("lpbs", [64, C], BF16))
        ptbuf = p3.enter_context(nc.sbuf_tensor("ptbuf", [128, 32 * 1024], BF16))
        vbuf = p3.enter_context(nc.sbuf_tensor("vbuf", [128, 3 * 512], BF16))
        aout = p3.enter_context(nc.sbuf_tensor("aout", [128, 8 * 1024], BF16))
        y1 = p3.enter_context(nc.sbuf_tensor("y1", [128, 8 * 1024], BF16))
        vres = p3.enter_context(nc.sbuf_tensor("vres", [128, 2 * 1024], F32))
        xtile = p3.enter_context(nc.sbuf_tensor("xtile", [128, 2 * 1024], F32))
        x2t = p3.enter_context(nc.sbuf_tensor("x2t", [128, 2 * 1024], F32))
        q_u = p3.enter_context(nc.sbuf_tensor("q_u", [128, 1024], F32))
        q_qg = p3.enter_context(nc.sbuf_tensor("q_qg", [128, 1024], BF16))
        q_t0 = p3.enter_context(nc.sbuf_tensor("q_t0", [128, 1024], BF16))
        q_sg = p3.enter_context(nc.sbuf_tensor("q_sg", [128, 1024], BF16))
        q_bm = p3.enter_context(nc.sbuf_tensor("q_bm", [128, 64], F32))
        q_sc = p3.enter_context(nc.sbuf_tensor("q_sc", [128, 64], F32))
        q_rs = p3.enter_context(nc.sbuf_tensor("q_rs", [128, 64], F32))
        q_out = p3.enter_context(nc.sbuf_tensor("q_out", [128, 2 * 1024], BF16))
        lnt = p3.enter_context(nc.sbuf_tensor("lnt", [128, 16], F32))

        @blk.sync
        def _(sy):
            sy.wait_ge(sm.h, P2_SM_END)
            sy.wait_ge(st.h, P_STORED)
            for c in range(8):
                sy.dma_start(wpt[:, c * C:(c + 1) * C],
                             wpb.ap()[0:C, c * 128:(c + 1) * 128],
                             transpose=True).then_inc(st.h, 16)
            sy.dma_start(apt[:, :], lpa_b.ap()[0:C, 0:128], transpose=True).then_inc(st.h, 16)
            sy.dma_start(lpbs[:, :], lpb_b.ap()).then_inc(st.h, 16)
        st.n += 10 * 16
        WP_L = st.n

        with (nc.psum_tensor("psp0", [128, 512], F32) as psp0,
              nc.psum_tensor("psp1", [128, 512], F32) as psp1):
            psp = [psp0, psp1]
            pgate = Gate(sv, 2)
            for idx in range(16):
                jc, hf = idx // 2, idx % 2
                slot = idx % 2
                pb = psp[slot]

                @blk.tensor
                def _(t, pb=pb, idx=idx, jc=jc, hf=hf, slot=slot):
                    if idx == 0:
                        t.wait_ge(st.h, WP_L)
                    pgate.wait(t, slot)
                    ii = t.matmul(pb[:, :], lpbs[0:64, jc * 128:(jc + 1) * 128],
                                  apt[0:64, hf * 512:(hf + 1) * 512],
                                  start=True, stop=True)
                    sm.inc(ii)

                @blk.vector
                def _(v, pb=pb, jc=jc, hf=hf, slot=slot):
                    v.wait_ge(sm.h, sm.n)
                    ii = v.tensor_tensor(
                        wmpt[:, jc * C + hf * 512:jc * C + (hf + 1) * 512],
                        wpt[:, jc * C + hf * 512:jc * C + (hf + 1) * 512],
                        pb[:, :], op=ALU.add)
                    sv.inc(ii)
                    pgate.release(slot)
        WMPT_V = sv.n

        with (nc.psum_tensor("pv0", [128, 512], F32) as pv0,
              nc.psum_tensor("pv1", [128, 512], F32) as pv1,
              nc.psum_tensor("pv2", [128, 512], F32) as pv2,
              nc.psum_tensor("pv3", [128, 512], F32) as pv3,
              nc.psum_tensor("pv4", [128, 512], F32) as pv4,
              nc.psum_tensor("pv5", [128, 512], F32) as pv5,
              nc.psum_tensor("pv6", [128, 512], F32) as pv6,
              nc.psum_tensor("pv7", [128, 512], F32) as pv7):
            pvb = [pv0, pv1, pv2, pv3, pv4, pv5, pv6, pv7]
            vslotg = Gate(sm, 3)
            pvg = Gate(sv, 8)
            p1g = Gate(se, 8)
            p2g = Gate(sv, 8)
            xtg = Gate(sv, 2)
            x2g = Gate(ss, 2)

            PV_SM_END = [0]
            for half in range(2):
                HALF_SV = sv.n
                HALF_SE = se.n

                @blk.sync
                def _(sy, half=half):
                    if half == 0:
                        sy.wait_ge(st.h, P_STORED)
                    else:
                        sy.wait_ge(sm.h, PV_SM_END[0])
                    for kc in range(32):
                        sy.dma_start(ptbuf[:, kc * 1024:(kc + 1) * 1024],
                                     p_dram.ap()[half * 1024:(half + 1) * 1024, kc * 128:(kc + 1) * 128],
                                     transpose=True).then_inc(st.h, 16)
                st.n += 32 * 16

                for cg in range(2):
                    for kc in range(32):
                        vslot = kc % 3

                        @blk.sync
                        def _(sy, half=half, cg=cg, kc=kc, vslot=vslot):
                            if half == 0 and cg == 0 and kc == 0:
                                sy.wait_ge(st.h, KV_STORED)
                            vslotg.wait(sy, vslot)
                            sy.dma_start(vbuf[:, vslot * 512:(vslot + 1) * 512],
                                         v_dram.ap()[kc * 128:(kc + 1) * 128, cg * 512:(cg + 1) * 512]
                                         ).then_inc(st.h, 16)
                        st.n += 16

                        @blk.tensor
                        def _(t, cg=cg, kc=kc, vslot=vslot, HALF_SV=HALF_SV, HALF_SE=HALF_SE):
                            t.wait_ge(st.h, st.n)
                            if kc == 0:
                                t.wait_ge(sv.h, HALF_SV)
                                t.wait_ge(se.h, HALF_SE)
                                for bb in range(8):
                                    pvg.wait(t, bb)
                            for cc in range(4):
                                for qq in range(2):
                                    ii = t.matmul(pvb[cc * 2 + qq][:, :],
                                                  vbuf[:, vslot * 512 + cc * 128:vslot * 512 + (cc + 1) * 128],
                                                  ptbuf[:, kc * 1024 + qq * 512:kc * 1024 + (qq + 1) * 512],
                                                  start=(kc == 0), stop=(kc == 31))
                            sm.inc(ii)
                            vslotg.release(vslot)

                    @blk.vector
                    def _(v, cg=cg):
                        v.wait_ge(sm.h, sm.n)
                        for cc in range(4):
                            cglob = cg * 4 + cc
                            for qq in range(2):
                                ii = v.tensor_scalar(
                                    aout[:, cglob * 1024 + qq * 512:cglob * 1024 + (qq + 1) * 512],
                                    pvb[cc * 2 + qq][:, :],
                                    stA[:, 16 + cglob:17 + cglob], stB[:, 16 + cglob:17 + cglob],
                                    op0=ALU.mult, op1=ALU.add)
                        sv.inc(ii)
                        for bb in range(8):
                            pvg.release(bb)
                AOUT_V = sv.n
                PV_SM_END[0] = sm.n

                for cp in range(8):
                    b0, b1 = (cp * 2) % 8, (cp * 2 + 1) % 8

                    @blk.tensor
                    def _(t, cp=cp, b0=b0, b1=b1, AOUT_V=AOUT_V):
                        if cp == 0:
                            t.wait_ge(sv.h, AOUT_V)
                            t.wait_ge(st.h, WP_L)
                        p1g.wait(t, b0)
                        p1g.wait(t, b1)
                        for qq in range(2):
                            pb = pvb[b0 if qq == 0 else b1]
                            for cc in range(8):
                                ii = t.matmul(pb[:, :],
                                              wpt[:, cc * C + cp * 128:cc * C + (cp + 1) * 128],
                                              aout[:, cc * 1024 + qq * 512:cc * 1024 + (qq + 1) * 512],
                                              start=(cc == 0), stop=(cc == 7))
                        sm.inc(ii)

                    @blk.scalar
                    def _(s, cp=cp, b0=b0, b1=b1):
                        s.wait_ge(sm.h, sm.n)
                        s.activation(y1[:, cp * 1024:cp * 1024 + 512], pvb[b0][:, :], AF.Copy)
                        ii = s.activation(y1[:, cp * 1024 + 512:(cp + 1) * 1024], pvb[b1][:, :], AF.Copy)
                        se.inc(ii)
                        p1g.release(b0)
                        p1g.release(b1)
                Y1_S = se.n

                for tt in range(8):
                    gt = half * 8 + tt
                    sl = tt % 2
                    b0, b1 = (tt * 2) % 8, (tt * 2 + 1) % 8

                    @blk.scalar
                    def _(s, gt=gt, sl=sl, half=half, tt=tt):
                        if half == 0 and tt == 0:
                            s.wait_ge(st.h, P_STORED)
                        xtg.wait(s, sl)
                        s.dma_start(out=xtile[:, sl * 1024:(sl + 1) * 1024],
                                    in_=xin.ap()[gt * 128:(gt + 1) * 128, :]).then_inc(ss.h, 16)
                    ss.n += 16
                    XT_L = ss.n

                    @blk.tensor
                    def _(t, tt=tt, b0=b0, b1=b1, Y1_S=Y1_S):
                        if tt == 0:
                            t.wait_ge(se.h, Y1_S)
                        p2g.wait(t, b0)
                        p2g.wait(t, b1)
                        for dg in range(2):
                            pb = pvb[b0 if dg == 0 else b1]
                            for cc in range(8):
                                ii = t.matmul(pb[:, :],
                                              y1[:, cc * 1024 + tt * 128:cc * 1024 + (tt + 1) * 128],
                                              wmpt[:, cc * C + dg * 512:cc * C + (dg + 1) * 512],
                                              start=(cc == 0), stop=(cc == 7))
                        sm.inc(ii)

                    @blk.vector
                    def _(v, sl=sl, b0=b0, b1=b1, XT_L=XT_L):
                        v.wait_ge(sm.h, sm.n)
                        v.wait_ge(ss.h, XT_L)
                        x2g.wait(v, sl)
                        v.tensor_tensor(vres[:, sl * 1024:sl * 1024 + 512], pvb[b0][:, :],
                                        xtile[:, sl * 1024:sl * 1024 + 512], op=ALU.add)
                        ii = v.tensor_tensor(vres[:, sl * 1024 + 512:(sl + 1) * 1024], pvb[b1][:, :],
                                             xtile[:, sl * 1024 + 512:(sl + 1) * 1024], op=ALU.add)
                        sv.inc(ii)
                        p2g.release(b0)
                        p2g.release(b1)
                        xtg.release(sl)

                    emit_ln(blk, sv, se, vres, x2t, lnt, sl, sqdump)

                    @blk.scalar
                    def _(s, gt=gt, sl=sl):
                        s.wait_ge(sv.h, sv.n)
                        s.dma_start(out=x2_dram.ap()[gt * 128:(gt + 1) * 128, :],
                                    in_=x2t[:, sl * 1024:(sl + 1) * 1024]).then_inc(ss.h, 16)
                    ss.n += 16

                    emit_quant(blk, sv, se, x2t, sl * 1024, q_u, q_qg, q_t0, q_sg,
                               q_bm, q_sc, q_rs, q_out, sl * 1024, 1024)

                    @blk.scalar
                    def _(s, gt=gt, sl=sl):
                        s.wait_ge(sv.h, sv.n)
                        s.dma_start(out=xq_dram.ap()[gt * 128:(gt + 1) * 128, :],
                                    in_=q_out[:, sl * 1024:(sl + 1) * 1024]).then_inc(ss.h, 16)
                    ss.n += 16
                    x2g.release(sl)
        XQ_STORED = ss.n
        p3.close()

        p4 = ctx.enter_context(contextlib.ExitStack())
        xqfm = p4.enter_context(nc.sbuf_tensor("xqfm", [128, 8 * NQ], BF16))
        wgt = p4.enter_context(nc.sbuf_tensor("wgt", [128, 8 * I], BF16))
        wut = p4.enter_context(nc.sbuf_tensor("wut", [128, 8 * I], BF16))
        silu = p4.enter_context(nc.sbuf_tensor("silu", [128, 512], F32))
        htile = p4.enter_context(nc.sbuf_tensor("htile", [128, 2 * I], F32))
        h_u = p4.enter_context(nc.sbuf_tensor("h_u", [128, I], F32))
        h_qg = p4.enter_context(nc.sbuf_tensor("h_qg", [128, I], BF16))
        h_t0 = p4.enter_context(nc.sbuf_tensor("h_t0", [128, I], BF16))
        h_sg = p4.enter_context(nc.sbuf_tensor("h_sg", [128, I], BF16))
        h_bm = p4.enter_context(nc.sbuf_tensor("h_bm", [128, I // 16], F32))
        h_sc = p4.enter_context(nc.sbuf_tensor("h_sc", [128, I // 16], F32))
        h_rs = p4.enter_context(nc.sbuf_tensor("h_rs", [128, I // 16], F32))
        h_out = p4.enter_context(nc.sbuf_tensor("h_out", [128, 2 * I], BF16))

        @blk.sync
        def _(sy):
            sy.wait_ge(ss.h, XQ_STORED)
            sy.wait_ge(scc.h, AG_DONE)
            for c in range(8):
                sy.dma_start(xqfm[:, c * NQ:(c + 1) * NQ],
                             xq_dram.ap()[0:NQ, c * 128:(c + 1) * 128],
                             transpose=True).then_inc(st.h, 16)
            for c in range(8):
                sy.dma_start(wgt[:, c * I:(c + 1) * I],
                             wgq.ap()[0:I, c * 128:(c + 1) * 128],
                             transpose=True).then_inc(st.h, 16)
            for c in range(8):
                sy.dma_start(wut[:, c * I:(c + 1) * I],
                             wuq.ap()[0:I, c * 128:(c + 1) * 128],
                             transpose=True).then_inc(st.h, 16)
        st.n += 24 * 16
        FFNW_L = st.n

        IGRP = [(0, 512), (512, 512), (1024, 512), (1536, 512), (2048, 512), (2560, 256)]
        with (nc.psum_tensor("fg0", [128, 512], F32) as fg0,
              nc.psum_tensor("fg1", [128, 512], F32) as fg1,
              nc.psum_tensor("fu0", [128, 512], F32) as fu0,
              nc.psum_tensor("fu1", [128, 512], F32) as fu1):
            fgb, fub = [fg0, fg1], [fu0, fu1]
            fgate = Gate(sv, 2)
            hqg = Gate(ss, 2)
            h_gprev = [0]

            for tile in range(16):
                sl = tile % 2
                for gi, (ib, iw) in enumerate(IGRP):
                    pb = gi % 2

                    @blk.tensor
                    def _(t, tile=tile, gi=gi, ib=ib, iw=iw, pb=pb):
                        if tile == 0 and gi == 0:
                            t.wait_ge(st.h, FFNW_L - 8 * 16)
                        fgate.wait(t, pb)
                        for cc in range(8):
                            ii = t.matmul(fgb[pb][:, 0:iw],
                                          xqfm[:, cc * NQ + tile * 128:cc * NQ + (tile + 1) * 128],
                                          wgt[:, cc * I + ib:cc * I + ib + iw],
                                          start=(cc == 0), stop=(cc == 7))
                        sm.inc(ii)
                        if tile == 0 and gi == 0:
                            t.wait_ge(st.h, FFNW_L)
                        for cc in range(8):
                            ii = t.matmul(fub[pb][:, 0:iw],
                                          xqfm[:, cc * NQ + tile * 128:cc * NQ + (tile + 1) * 128],
                                          wut[:, cc * I + ib:cc * I + ib + iw],
                                          start=(cc == 0), stop=(cc == 7))
                        sm.inc(ii)

                    @blk.scalar
                    def _(s, gi=gi, pb=pb, iw=iw):
                        s.wait_ge(sm.h, sm.n)
                        ii = s.activation(silu[:, 0:iw], fgb[pb][:, 0:iw], AF.Silu)
                        se.inc(ii)

                    @blk.vector
                    def _(v, sl=sl, ib=ib, iw=iw, pb=pb, gi=gi):
                        v.wait_ge(se.h, se.n)
                        if gi == 0:
                            hqg.wait(v, sl)
                        ii = v.tensor_tensor(htile[:, sl * I + ib:sl * I + ib + iw],
                                             silu[:, 0:iw], fub[pb][:, 0:iw], op=ALU.mult)
                        sv.inc(ii)
                        fgate.release(pb)

                emit_quant_sp(blk, sv, se, sg, h_gprev, htile, sl * I, h_u, h_qg,
                              h_t0, h_sg, h_bm, h_sc, h_rs, h_out, sl * I, I, 2048)

                @blk.scalar
                def _(s, tile=tile, sl=sl):
                    s.wait_ge(sv.h, sv.n)
                    s.wait_ge(sg.h, sg.n)
                    s.dma_start(out=hq_dram.ap()[tile * 128:(tile + 1) * 128, :],
                                in_=h_out[:, sl * I:(sl + 1) * I]).then_inc(ss.h, 16)
                ss.n += 16
                hqg.release(sl)
        HQ_STORED = ss.n
        p4.close()

        p5 = ctx.enter_context(contextlib.ExitStack())
        hqfm = p5.enter_context(nc.sbuf_tensor("hqfm", [128, 22 * NQ], BF16))
        wdt = p5.enter_context(nc.sbuf_tensor("wdt", [128, 22 * C], BF16))
        x2tile = p5.enter_context(nc.sbuf_tensor("x2tile", [128, 2 * C], F32))
        v2res = p5.enter_context(nc.sbuf_tensor("v2res", [128, 2 * C], F32))
        otile = p5.enter_context(nc.sbuf_tensor("otile", [128, 2 * C], F32))
        ln2 = p5.enter_context(nc.sbuf_tensor("ln2", [128, 16], F32))

        @blk.sync
        def _(sy):
            sy.wait_ge(ss.h, HQ_STORED)
            for c in range(22):
                sy.dma_start(hqfm[:, c * NQ:(c + 1) * NQ],
                             hq_dram.ap()[0:NQ, c * 128:(c + 1) * 128],
                             transpose=True).then_inc(st.h, 16)
            for c in range(22):
                sy.dma_start(wdt[:, c * C:(c + 1) * C],
                             wdq.ap()[0:C, c * 128:(c + 1) * 128],
                             transpose=True).then_inc(st.h, 16)
        st.n += 44 * 16
        P5_L = st.n

        with (nc.psum_tensor("dn0", [128, 512], F32) as dn0,
              nc.psum_tensor("dn1", [128, 512], F32) as dn1,
              nc.psum_tensor("dn2", [128, 512], F32) as dn2,
              nc.psum_tensor("dn3", [128, 512], F32) as dn3):
            dnb = [dn0, dn1, dn2, dn3]
            dgate = Gate(sv, 4)
            xslot = Gate(sv, 2)
            ygate = Gate(so, 2)

            for tile in range(16):
                sl = tile % 2
                b0, b1 = sl * 2, sl * 2 + 1

                @blk.scalar
                def _(s, tile=tile, sl=sl):
                    xslot.wait(s, sl)
                    s.dma_start(out=x2tile[:, sl * C:(sl + 1) * C],
                                in_=x2_dram.ap()[tile * 128:(tile + 1) * 128, :]).then_inc(ss.h, 16)
                ss.n += 16
                X2T_L = ss.n

                @blk.tensor
                def _(t, tile=tile, b0=b0, b1=b1):
                    if tile == 0:
                        t.wait_ge(st.h, P5_L)
                    dgate.wait(t, b0)
                    dgate.wait(t, b1)
                    for dg in range(2):
                        pb = dnb[b0 if dg == 0 else b1]
                        for cc in range(22):
                            ii = t.matmul(pb[:, :],
                                          hqfm[:, cc * NQ + tile * 128:cc * NQ + (tile + 1) * 128],
                                          wdt[:, cc * C + dg * 512:cc * C + (dg + 1) * 512],
                                          start=(cc == 0), stop=(cc == 21))
                    sm.inc(ii)

                @blk.vector
                def _(v, sl=sl, b0=b0, b1=b1, X2T_L=X2T_L):
                    v.wait_ge(sm.h, sm.n)
                    v.wait_ge(ss.h, X2T_L)
                    ygate.wait(v, sl)
                    v.tensor_tensor(v2res[:, sl * C:sl * C + 512], dnb[b0][:, :],
                                    x2tile[:, sl * C:sl * C + 512], op=ALU.add)
                    ii = v.tensor_tensor(v2res[:, sl * C + 512:(sl + 1) * C], dnb[b1][:, :],
                                         x2tile[:, sl * C + 512:(sl + 1) * C], op=ALU.add)
                    sv.inc(ii)
                    dgate.release(b0)
                    dgate.release(b1)
                    xslot.release(sl)

                emit_ln(blk, sv, se, v2res, otile, ln2, sl, sqdump)

                @blk.scalar
                def _(s, tile=tile, sl=sl):
                    s.wait_ge(sv.h, sv.n)
                    s.dma_start(out=yout.ap()[tile * 128:(tile + 1) * 128, :],
                                in_=otile[:, sl * C:(sl + 1) * C]).then_inc(so.h, 16)
                so.n += 16
                ygate.release(sl)

        @blk.scalar
        def _(s):
            s.wait_ge(so.h, so.n)

    return nc


_CACHE = {}


def kernel(L0_attn_x, L0_attn_W_attn, L0_attn_W_proj, L0_attn_lora_attn_A,
           L0_attn_lora_attn_B, L0_attn_lora_proj_A, L0_attn_lora_proj_B,
           L0_attn_bn_gamma, L0_attn_bn_beta, L0_ffn_w_gate, L0_ffn_w_up,
           L0_ffn_w_down):
    x = np.ascontiguousarray(np.asarray(L0_attn_x, dtype=np.float32))
    wa = np.ascontiguousarray(np.asarray(L0_attn_W_attn, dtype=np.float32))
    wp = np.ascontiguousarray(np.asarray(L0_attn_W_proj, dtype=np.float32))
    A1 = np.ascontiguousarray(np.asarray(L0_attn_lora_attn_A, dtype=np.float32))
    B1 = np.ascontiguousarray(np.asarray(L0_attn_lora_attn_B, dtype=np.float32))
    A2 = np.ascontiguousarray(np.asarray(L0_attn_lora_proj_A, dtype=np.float32))
    B2 = np.ascontiguousarray(np.asarray(L0_attn_lora_proj_B, dtype=np.float32))
    gam = np.asarray(L0_attn_bn_gamma, dtype=np.float32)
    bet = np.asarray(L0_attn_bn_beta, dtype=np.float32)
    wg = np.ascontiguousarray(np.asarray(L0_ffn_w_gate, dtype=np.float32))
    wu = np.ascontiguousarray(np.asarray(L0_ffn_w_up, dtype=np.float32))
    wd = np.ascontiguousarray(np.asarray(L0_ffn_w_down, dtype=np.float32))

    if "nc" not in _CACHE:
        _CACHE["nc"] = build_nc()
    nc = _CACHE["nc"]

    gam2d = np.ascontiguousarray(gam.reshape(24, 128).T)
    bet2d = np.ascontiguousarray(bet.reshape(24, 128).T)
    ones = np.ones((128, 1), dtype=np.float32)
    shsz = 128 * SHF

    in_maps = []
    for c in range(NCORE):
        s, h = c % 4, c // 4
        own = x[s, h * NQ:(h + 1) * NQ, :]
        sib = x[s, (1 - h) * NQ:(2 - h) * NQ, :]
        xp = np.ascontiguousarray(np.concatenate([own, sib], axis=0))
        kidx_row = np.concatenate([np.arange(h * NQ, (h + 1) * NQ),
                                   np.arange((1 - h) * NQ, (2 - h) * NQ)]).astype(np.float32)
        kidxp = np.ascontiguousarray(np.broadcast_to(kidx_row, (128, TOK)))
        qp = (h * NQ + (np.arange(16) * 128)[None, :] + np.arange(128)[:, None]).astype(np.float32)
        in_maps.append({
            "x": xp,
            "w_attn": wa, "w_proj": wp,
            "laa": A1, "lab": B1, "lpa": A2, "lpb": B2,
            "gam2d": gam2d, "bet2d": bet2d,
            "wg_sh": np.ascontiguousarray(wg.reshape(-1)[c * shsz:(c + 1) * shsz].reshape(128, SHF)),
            "wu_sh": np.ascontiguousarray(wu.reshape(-1)[c * shsz:(c + 1) * shsz].reshape(128, SHF)),
            "wd_sh": np.ascontiguousarray(wd.reshape(-1)[c * shsz:(c + 1) * shsz].reshape(128, SHF)),
            "kidx": kidxp, "qpos": np.ascontiguousarray(qp), "ones": ones,
        })

    res = run_bass_kernel_spmd(nc, in_maps, core_ids=list(range(NCORE)))

    out = np.empty((B, T, C), dtype=np.float32)
    for c in range(NCORE):
        s, h = c % 4, c // 4
        out[s, h * NQ:(h + 1) * NQ, :] = res.results[c]["y"]
    return out


# version 13
# speedup vs baseline: 1.5808x; 1.5808x over previous
import numpy as np
import concourse.bass as bass
import concourse.mybir as mybir
from concourse.bass_utils import run_bass_kernel_spmd

F32 = mybir.dt.float32
BF16 = mybir.dt.bfloat16
AF = mybir.ActivationFunctionType
ALU = mybir.AluOpType
AXX = mybir.AxisListType.X

B, T, C, D3, R, I = 4, 4096, 1024, 3072, 64, 2816
EPS = 1e-5
TOK = 4096
NQ = 2048
NCORE = 8
SHF = (I * C) // NCORE // 128
SHQ = SHF // 4
NEGBIG = -10000.0
M2 = [0.0625, 0.5625, 1.5625, 3.0625, 6.25, 12.25, 25.0]
WTS = [0.5, 0.5, 0.5, 0.5, 1.0, 1.0, 2.0]
INV1024 = 1.0 / 1024.0
INV16384 = 1.0 / 16384.0
SMSCALE = 0.03125


class Sem:
    __slots__ = ("h", "n")

    def __init__(self, h):
        self.h = h
        self.n = 0

    def inc(self, ins, k=1):
        ins.then_inc(self.h, k)
        self.n += k
        return self.n


class Gate:

    def __init__(self, sem, nslots):
        self.sem = sem
        self.v = [0] * nslots

    def wait(self, eng, slot):
        if self.v[slot] > 0:
            eng.wait_ge(self.sem.h, self.v[slot])

    def release(self, slot):
        self.v[slot] = self.sem.n


def emit_quant(blk, sv, se, src, osrc, u, qg, t0, sgn, bm, sc, rs, out, oout, n):
    nb = n // 16

    @blk.vector
    def _(v):
        i = v.tensor_reduce(bm[:, 0:nb, None],
                            src.ap()[:, osrc:osrc + n].rearrange("p (a b) -> p a b", b=16),
                            axis=AXX, op=ALU.max, apply_absolute_value=True)
        sv.inc(i)
        v.wait_ge(sv.h, sv.n)
        i = v.tensor_scalar(sc[:, 0:nb], bm[:, 0:nb], 1.0 / 6.0, 1e-12,
                            op0=ALU.mult, op1=ALU.max)
        sv.inc(i)
        v.wait_ge(sv.h, sv.n)
        i = v.reciprocal(rs[:, 0:nb], sc[:, 0:nb])
        sv.inc(i)
        v.wait_ge(sv.h, sv.n)
        i = v.tensor_tensor(u.ap()[:, 0:n].rearrange("p (a b) -> p a b", b=16),
                            src.ap()[:, osrc:osrc + n].rearrange("p (a b) -> p a b", b=16),
                            rs[:, 0:nb, None].broadcast_to((128, nb, 16)),
                            op=ALU.mult)
        sv.inc(i)

    @blk.scalar
    def _(s):
        s.wait_ge(sv.h, sv.n)
        i = s.sign(sgn[:, 0:n], u[:, 0:n])
        se.inc(i)
        s.wait_ge(se.h, se.n)
        i = s.square(u[:, 0:n], u[:, 0:n])
        se.inc(i)

    @blk.vector
    def _(v):
        v.wait_ge(se.h, se.n)
        v.tensor_scalar(qg[:, 0:n], u[:, 0:n], M2[0], WTS[0],
                        op0=ALU.is_gt, op1=ALU.mult)
        for j in range(1, 7):
            v.tensor_scalar(t0[:, 0:n], u[:, 0:n], M2[j], WTS[j],
                            op0=ALU.is_gt, op1=ALU.mult)
            v.tensor_tensor(qg[:, 0:n], qg[:, 0:n], t0[:, 0:n], op=ALU.add)
        v.tensor_tensor(qg[:, 0:n], qg[:, 0:n], sgn[:, 0:n], op=ALU.mult)
        i = v.tensor_tensor(out.ap()[:, oout:oout + n].rearrange("p (a b) -> p a b", b=16),
                            qg.ap()[:, 0:n].rearrange("p (a b) -> p a b", b=16),
                            sc[:, 0:nb, None].broadcast_to((128, nb, 16)),
                            op=ALU.mult)
        sv.inc(i)


def emit_ln(blk, sv, se, vres, outt, lnt, sl, sqdump):
    a, b_, c_, d_ = sl * 4, sl * 4 + 1, sl * 4 + 2, sl * 4 + 3
    o = sl * 1024

    @blk.vector
    def _(v):
        i = v.tensor_reduce(lnt[:, a:a + 1], vres[:, o:o + 1024], axis=AXX, op=ALU.add)
        sv.inc(i)

    @blk.scalar
    def _(s):
        s.wait_ge(sv.h, sv.n)
        i = s.activation(sqdump[:, :], vres[:, o:o + 1024], AF.Square,
                         accum_out=lnt[:, b_:b_ + 1])
        se.inc(i)

    @blk.vector
    def _(v):
        v.wait_ge(se.h, se.n)
        i = v.tensor_scalar(lnt[:, a:a + 1], lnt[:, a:a + 1], INV1024, None, op0=ALU.mult)
        sv.inc(i)
        v.wait_ge(sv.h, sv.n)
        i = v.tensor_scalar(lnt[:, b_:b_ + 1], lnt[:, b_:b_ + 1], INV1024, None, op0=ALU.mult)
        sv.inc(i)
        v.wait_ge(sv.h, sv.n)
        i = v.tensor_tensor(lnt[:, c_:c_ + 1], lnt[:, a:a + 1], lnt[:, a:a + 1], op=ALU.mult)
        sv.inc(i)
        v.wait_ge(sv.h, sv.n)
        i = v.tensor_tensor(lnt[:, b_:b_ + 1], lnt[:, b_:b_ + 1], lnt[:, c_:c_ + 1],
                            op=ALU.subtract)
        sv.inc(i)
        v.wait_ge(sv.h, sv.n)
        i = v.tensor_scalar(lnt[:, b_:b_ + 1], lnt[:, b_:b_ + 1], EPS, None, op0=ALU.add)
        sv.inc(i)

    @blk.scalar
    def _(s):
        s.wait_ge(sv.h, sv.n)
        i = s.activation(lnt[:, c_:c_ + 1], lnt[:, b_:b_ + 1], AF.Sqrt)
        se.inc(i)

    @blk.vector
    def _(v):
        v.wait_ge(se.h, se.n)
        i = v.reciprocal(lnt[:, d_:d_ + 1], lnt[:, c_:c_ + 1])
        sv.inc(i)
        v.wait_ge(sv.h, sv.n)
        i = v.tensor_tensor(lnt[:, c_:c_ + 1], lnt[:, a:a + 1], lnt[:, d_:d_ + 1],
                            op=ALU.mult)
        sv.inc(i)
        v.wait_ge(sv.h, sv.n)
        i = v.tensor_scalar(lnt[:, c_:c_ + 1], lnt[:, c_:c_ + 1], -1.0, None, op0=ALU.mult)
        sv.inc(i)
        v.wait_ge(sv.h, sv.n)
        i = v.tensor_scalar(outt[:, o:o + 1024], vres[:, o:o + 1024],
                            lnt[:, d_:d_ + 1], lnt[:, c_:c_ + 1],
                            op0=ALU.mult, op1=ALU.add)
        sv.inc(i)


def build_nc():
    import contextlib
    nc = bass.Bass("TRN2", target_bir_lowering=False)

    xin = nc.dram_tensor("x", [TOK, C], F32, kind="ExternalInput")
    w_attn = nc.dram_tensor("w_attn", [D3, C], F32, kind="ExternalInput")
    w_proj = nc.dram_tensor("w_proj", [C, C], F32, kind="ExternalInput")
    laa = nc.dram_tensor("laa", [C, R], F32, kind="ExternalInput")
    lab = nc.dram_tensor("lab", [R, D3], F32, kind="ExternalInput")
    lpa = nc.dram_tensor("lpa", [C, R], F32, kind="ExternalInput")
    lpb = nc.dram_tensor("lpb", [R, C], F32, kind="ExternalInput")
    gam2d = nc.dram_tensor("gam2d", [128, 24], F32, kind="ExternalInput")
    bet2d = nc.dram_tensor("bet2d", [128, 24], F32, kind="ExternalInput")
    wg_sh = nc.dram_tensor("wg_sh", [128, SHF], F32, kind="ExternalInput")
    wu_sh = nc.dram_tensor("wu_sh", [128, SHF], F32, kind="ExternalInput")
    wd_sh = nc.dram_tensor("wd_sh", [128, SHF], F32, kind="ExternalInput")
    kidx_in = nc.dram_tensor("kidx", [128, TOK], F32, kind="ExternalInput")
    qpos_in = nc.dram_tensor("qpos", [128, 16], F32, kind="ExternalInput")
    ones_in = nc.dram_tensor("ones", [128, 1], F32, kind="ExternalInput")
    yout = nc.dram_tensor("y", [NQ, C], F32, kind="ExternalOutput")

    xp_b = nc.dram_tensor("xp_b", [TOK, C], BF16)
    wab = nc.dram_tensor("wab", [D3, C], BF16)
    wpb = nc.dram_tensor("wpb", [C, C], BF16)
    laa_b = nc.dram_tensor("laa_b", [C, 128], BF16)
    lpa_b = nc.dram_tensor("lpa_b", [C, 128], BF16)
    lab_b = nc.dram_tensor("lab_b", [R, D3], BF16)
    lpb_b = nc.dram_tensor("lpb_b", [R, C], BF16)
    k_dram = nc.dram_tensor("k_dram", [C, TOK], BF16)
    v_dram = nc.dram_tensor("v_dram", [TOK, C], BF16)
    p_dram = nc.dram_tensor("p_dram", [NQ, TOK], BF16)
    xq_dram = nc.dram_tensor("xq_dram", [NQ, C], BF16)
    hq_dram = nc.dram_tensor("hq_dram", [NQ, I], BF16)
    x2_dram = nc.dram_tensor("x2_dram", [NQ, C], F32)
    agi_g = nc.dram_tensor("agi_g", [128, SHF], BF16)
    agi_u = nc.dram_tensor("agi_u", [128, SHF], BF16)
    agi_d = nc.dram_tensor("agi_d", [128, SHF], BF16)
    wgq = nc.dram_tensor("wgq", [I, C], BF16)
    wuq = nc.dram_tensor("wuq", [I, C], BF16)
    wdq = nc.dram_tensor("wdq", [C, I], BF16)
    ccin = nc.dram_tensor("ccin", [128, 48], F32)
    ccout = nc.dram_tensor("ccout", [128, 48], F32)

    ctx = contextlib.ExitStack()
    with ctx:
        sg = Sem(ctx.enter_context(nc.semaphore("sg")))
        st = Sem(ctx.enter_context(nc.semaphore("st")))
        ss = Sem(ctx.enter_context(nc.semaphore("ssd")))
        sm = Sem(ctx.enter_context(nc.semaphore("sm")))
        sv = Sem(ctx.enter_context(nc.semaphore("sv")))
        se = Sem(ctx.enter_context(nc.semaphore("se")))
        scc = Sem(ctx.enter_context(nc.semaphore("scc")))
        so = Sem(ctx.enter_context(nc.semaphore("so")))

        gamma = ctx.enter_context(nc.sbuf_tensor("gamma_sb", [128, 24], F32))
        beta = ctx.enter_context(nc.sbuf_tensor("beta_sb", [128, 24], F32))
        stA = ctx.enter_context(nc.sbuf_tensor("stA", [128, 24], F32))
        stB = ctx.enter_context(nc.sbuf_tensor("stB", [128, 24], F32))
        aqk = ctx.enter_context(nc.sbuf_tensor("aqk", [128, 8], F32))
        bqk = ctx.enter_context(nc.sbuf_tensor("bqk", [128, 8], F32))
        qpos = ctx.enter_context(nc.sbuf_tensor("qpos_sb", [128, 16], F32))
        onec = ctx.enter_context(nc.sbuf_tensor("onec", [128, 1], BF16))
        smx = ctx.enter_context(nc.sbuf_tensor("smx", [128, 64], F32))
        sqdump = ctx.enter_context(nc.sbuf_tensor("sqdump", [128, 1024], BF16))

        blk = ctx.enter_context(nc.Block())

        @blk.gpsimd
        def _(g):
            g.dma_start(out=wab.ap(), in_=w_attn.ap()).then_inc(sg.h, 16)
            g.dma_start(out=laa_b.ap()[:, 0:R], in_=laa.ap()).then_inc(sg.h, 16)
            g.dma_start(out=lab_b.ap(), in_=lab.ap()).then_inc(sg.h, 16)
        sg.n += 3 * 16
        WAB_C = sg.n

        @blk.gpsimd
        def _(g):
            g.dma_start(out=xp_b.ap(), in_=xin.ap()).then_inc(sg.h, 16)
        sg.n += 16
        XPB_C = sg.n

        @blk.gpsimd
        def _(g):
            g.dma_start(out=wpb.ap(), in_=w_proj.ap()).then_inc(sg.h, 16)
            g.dma_start(out=lpa_b.ap()[:, 0:R], in_=lpa.ap()).then_inc(sg.h, 16)
            g.dma_start(out=lpb_b.ap(), in_=lpb.ap()).then_inc(sg.h, 16)
            g.dma_start(out=onec[:, :], in_=ones_in.ap()).then_inc(sg.h, 16)
        sg.n += 4 * 16
        CASTS = sg.n

        @blk.scalar
        def _(s):
            s.dma_start(out=gamma[:, :], in_=gam2d.ap()).then_inc(ss.h, 16)
            s.dma_start(out=beta[:, :], in_=bet2d.ap()).then_inc(ss.h, 16)
            s.dma_start(out=qpos[:, :], in_=qpos_in.ap()).then_inc(ss.h, 16)
        ss.n += 3 * 16

        qstack = ctx.enter_context(contextlib.ExitStack())
        qraw = qstack.enter_context(nc.sbuf_tensor("qraw", [128, 8 * NQ], BF16))

        pA = qstack.enter_context(contextlib.ExitStack())
        xfm = pA.enter_context(nc.sbuf_tensor("xfm", [128, 8 * TOK], BF16))
        wmt = pA.enter_context(nc.sbuf_tensor("wmt", [128, 8 * D3], BF16))
        qsh = pA.enter_context(nc.sbuf_tensor("qsh", [128, SHQ], F32))
        qu = pA.enter_context(nc.sbuf_tensor("qu", [128, SHQ], F32))
        qg_ = pA.enter_context(nc.sbuf_tensor("qg_", [128, SHQ], BF16))
        qt_ = pA.enter_context(nc.sbuf_tensor("qt_", [128, SHQ], BF16))
        qs_ = pA.enter_context(nc.sbuf_tensor("qs_", [128, SHQ], BF16))
        qq_ = pA.enter_context(nc.sbuf_tensor("qq_", [128, SHQ], BF16))
        qbm = pA.enter_context(nc.sbuf_tensor("qbm", [128, SHQ // 16], F32))
        qsc = pA.enter_context(nc.sbuf_tensor("qsc", [128, SHQ // 16], F32))
        qrs = pA.enter_context(nc.sbuf_tensor("qrs", [128, SHQ // 16], F32))

        SHARD_SRCS = ((wg_sh, agi_g), (wu_sh, agi_u), (wd_sh, agi_d))

        def emit_shard_chunk(i):
            src_d, agi = SHARD_SRCS[i // 4]
            ch = i % 4

            @blk.scalar
            def _(s, src_d=src_d, ch=ch):
                s.wait_ge(sv.h, sv.n)
                s.dma_start(out=qsh[:, :],
                            in_=src_d.ap()[:, ch * SHQ:(ch + 1) * SHQ]).then_inc(ss.h, 16)
            ss.n += 16

            @blk.vector
            def _(v):
                v.wait_ge(ss.h, ss.n)

            @blk.scalar
            def _(s):
                s.wait_ge(ss.h, ss.n)

            emit_quant(blk, sv, se, qsh, 0, qu, qg_, qt_, qs_, qbm, qsc, qrs, qq_, 0, SHQ)

            @blk.scalar
            def _(s, agi=agi, ch=ch):
                s.wait_ge(sv.h, sv.n)
                s.dma_start(out=agi.ap()[:, ch * SHQ:(ch + 1) * SHQ],
                            in_=qq_[:, :]).then_inc(ss.h, 16)
            ss.n += 16

        aat = pA.enter_context(nc.sbuf_tensor("aat", [128, C], BF16))
        labs = pA.enter_context(nc.sbuf_tensor("labs", [64, D3], BF16))

        @blk.sync
        def _(sy):
            sy.wait_ge(sg.h, WAB_C)
            for c in range(8):
                sy.dma_start(wmt[:, c * D3:(c + 1) * D3],
                             wab.ap()[0:D3, c * 128:(c + 1) * 128],
                             transpose=True).then_inc(st.h, 16)
            sy.dma_start(aat[:, :], laa_b.ap()[0:C, 0:128], transpose=True).then_inc(st.h, 16)
            sy.dma_start(labs[:, :], lab_b.ap()).then_inc(st.h, 16)
            sy.wait_ge(sg.h, XPB_C)
            for c in range(8):
                sy.dma_start(xfm[:, c * TOK:(c + 1) * TOK],
                             xp_b.ap()[0:TOK, c * 128:(c + 1) * 128],
                             transpose=True).then_inc(st.h, 16)
        st.n += 18 * 16
        WLOADS = st.n

        with (nc.psum_tensor("psl0", [128, 512], F32) as psl0,
              nc.psum_tensor("psl1", [128, 512], F32) as psl1):
            psl = [psl0, psl1]
            lgate = Gate(sv, 2)
            for idx in range(48):
                jc, hf = idx // 2, idx % 2
                slot = idx % 2
                pb = psl[slot]

                @blk.tensor
                def _(t, pb=pb, idx=idx, jc=jc, hf=hf, slot=slot):
                    if idx == 0:
                        t.wait_ge(st.h, WLOADS - 8 * 16)
                    lgate.wait(t, slot)
                    ii = t.matmul(pb[:, :], labs[0:64, jc * 128:(jc + 1) * 128],
                                  aat[0:64, hf * 512:(hf + 1) * 512],
                                  start=True, stop=True)
                    sm.inc(ii)

                @blk.vector
                def _(v, pb=pb, jc=jc, hf=hf, slot=slot):
                    v.wait_ge(sm.h, sm.n)
                    s3, cch = jc // 8, jc % 8
                    o0 = cch * D3 + s3 + 3 * 512 * hf
                    tgt = wmt.ap()[:, o0:o0 + 3 * 511 + 1:3]
                    ii = v.tensor_tensor(tgt, tgt, pb[:, :], op=ALU.add)
                    sv.inc(ii)
                    lgate.release(slot)
        LORA_V = sv.n

        kvst = pA.enter_context(nc.sbuf_tensor("kvst", [128, 4 * 512], BF16))
        vsq = pA.enter_context(nc.sbuf_tensor("vsq", [128, 4 * 512], BF16))
        qp_s = pA.enter_context(nc.sbuf_tensor("qp_s", [128, 32], F32))
        qp_q = pA.enter_context(nc.sbuf_tensor("qp_q", [128, 32], F32))
        kp_s = pA.enter_context(nc.sbuf_tensor("kp_s", [128, 32], F32))
        kp_q = pA.enter_context(nc.sbuf_tensor("kp_q", [128, 32], F32))
        stat48 = pA.enter_context(nc.sbuf_tensor("stat48", [128, 48], F32))

        with (nc.psum_tensor("pr0", [128, 512], F32) as pr0,
              nc.psum_tensor("pr1", [128, 512], F32) as pr1,
              nc.psum_tensor("pr2", [128, 512], F32) as pr2,
              nc.psum_tensor("pr3", [128, 512], F32) as pr3,
              nc.psum_tensor("psv", [128, 16], F32) as psv):
            prr = [pr0, pr1, pr2, pr3]
            bgate = Gate(se, 4)
            vgate = Gate(sv, 4)
            kstg = Gate(st, 4)

            for it in range(64):
                tile, dg = it // 2, it % 2
                bank = it % 4
                stg = it % 4
                pb = prr[bank]

                @blk.tensor
                def _(t, pb=pb, tile=tile, dg=dg, bank=bank, it=it):
                    if it == 0:
                        t.wait_ge(sv.h, LORA_V)
                    vgate.wait(t, bank)
                    for cch in range(8):
                        ii = t.matmul(pb[:, :],
                                      xfm[:, cch * TOK + tile * 128:cch * TOK + (tile + 1) * 128],
                                      wmt[:, cch * D3 + 2048 + dg * 512:cch * D3 + 2048 + (dg + 1) * 512],
                                      start=(cch == 0), stop=(cch == 7))
                    sm.inc(ii)

                @blk.vector
                def _(v, pb=pb, stg=stg, bank=bank):
                    v.wait_ge(sm.h, sm.n)
                    kstg.wait(v, stg)
                    ii = v.tensor_copy(kvst[:, stg * 512:(stg + 1) * 512], pb[:, :])
                    sv.inc(ii)
                    vgate.release(bank)

                @blk.scalar
                def _(s, stg=stg, tile=tile, dg=dg):
                    s.wait_ge(sv.h, sv.n)
                    s.dma_start(out=v_dram.ap()[tile * 128:(tile + 1) * 128, dg * 512:(dg + 1) * 512],
                                in_=kvst[:, stg * 512:(stg + 1) * 512]).then_inc(st.h, 16)
                st.n += 16
                kstg.release(stg)
                if it % 4 == 0 and it // 4 < 12:
                    emit_shard_chunk(it // 4)

            AGI_STORED = ss.n

            @blk.gpsimd
            def _(g):
                g.wait_ge(ss.h, AGI_STORED)
                for i_, o_ in ((agi_g, wgq), (agi_u, wuq), (agi_d, wdq)):
                    ii = g.collective_compute("AllGather", ALU.bypass,
                                              replica_groups=[list(range(NCORE))],
                                              ins=[i_.ap()], outs=[o_.ap()])
                    scc.inc(ii)
            AG_DONE = scc.n

            for it in range(64):
                tg, d = it // 8, it % 8
                bank = it % 4
                stg = it % 4
                own = tg < 4
                pb = prr[bank]

                @blk.tensor
                def _(t, pb=pb, tg=tg, d=d, bank=bank):
                    bgate.wait(t, bank)
                    vgate.wait(t, bank)
                    for cch in range(8):
                        ii = t.matmul(pb[:, :],
                                      wmt[:, cch * D3 + 1024 + d * 128:cch * D3 + 1024 + (d + 1) * 128],
                                      xfm[:, cch * TOK + tg * 512:cch * TOK + (tg + 1) * 512],
                                      start=(cch == 0), stop=(cch == 7))
                    sm.inc(ii)

                if own:
                    @blk.scalar
                    def _(s, pb=pb, tg=tg, d=d, bank=bank, stg=stg):
                        s.wait_ge(sm.h, sm.n)
                        kstg.wait(s, stg)
                        s.activation(kvst[:, stg * 512:(stg + 1) * 512], pb[:, :], AF.Copy,
                                     accum_out=kp_s[:, d * 4 + tg:d * 4 + tg + 1])
                        ii = s.activation(sqdump[:, 0:512], pb[:, :], AF.Square,
                                          accum_out=kp_q[:, d * 4 + tg:d * 4 + tg + 1])
                        se.inc(ii)
                        bgate.release(bank)
                        s.wait_ge(se.h, se.n)
                        s.dma_start(out=k_dram.ap()[d * 128:(d + 1) * 128, tg * 512:(tg + 1) * 512],
                                    in_=kvst[:, stg * 512:(stg + 1) * 512]).then_inc(st.h, 16)
                else:
                    @blk.vector
                    def _(v, pb=pb, stg=stg, bank=bank):
                        v.wait_ge(sm.h, sm.n)
                        kstg.wait(v, stg)
                        ii = v.tensor_copy(kvst[:, stg * 512:(stg + 1) * 512], pb[:, :])
                        sv.inc(ii)
                        vgate.release(bank)

                    @blk.scalar
                    def _(s, tg=tg, d=d, stg=stg):
                        s.wait_ge(sv.h, sv.n)
                        s.dma_start(out=k_dram.ap()[d * 128:(d + 1) * 128, tg * 512:(tg + 1) * 512],
                                    in_=kvst[:, stg * 512:(stg + 1) * 512]).then_inc(st.h, 16)
                st.n += 16
                kstg.release(stg)

            VSTATS_ST = st.n
            sq_se = [0] * 32
            mm_sm = [0] * 32

            def emit_statsmm(step):
                ti, dg = step // 2, step % 2
                sl4 = step % 4

                @blk.tensor
                def _(t, ti=ti, dg=dg, sl4=sl4, step=step):
                    t.wait_ge(se.h, sq_se[step])
                    for sl in range(4):
                        t.matmul(psv[:, dg * 4 + sl:dg * 4 + sl + 1],
                                 kvst[:, sl4 * 512 + sl * 128:sl4 * 512 + (sl + 1) * 128],
                                 onec[:, 0:1],
                                 start=(ti == 0), stop=(ti == 15))
                        ii = t.matmul(psv[:, 8 + dg * 4 + sl:9 + dg * 4 + sl],
                                      vsq[:, sl4 * 512 + sl * 128:sl4 * 512 + (sl + 1) * 128],
                                      onec[:, 0:1],
                                      start=(ti == 0), stop=(ti == 15))
                    sm.inc(ii)
                mm_sm[step] = sm.n

            for it in range(32):
                tg, d = it // 8, it % 8
                bank = it % 4
                pb = prr[bank]

                @blk.tensor
                def _(t, pb=pb, tg=tg, d=d, bank=bank):
                    bgate.wait(t, bank)
                    vgate.wait(t, bank)
                    for cch in range(8):
                        ii = t.matmul(pb[:, :],
                                      wmt[:, cch * D3 + d * 128:cch * D3 + (d + 1) * 128],
                                      xfm[:, cch * TOK + tg * 512:cch * TOK + (tg + 1) * 512],
                                      start=(cch == 0), stop=(cch == 7))
                    sm.inc(ii)

                if it >= 4:
                    emit_statsmm(it - 4)

                step, sl4 = it, it % 4
                ti2, dg2 = step // 2, step % 2

                @blk.sync
                def _(sy, ti2=ti2, dg2=dg2, sl4=sl4, step=step):
                    if step < 4:
                        sy.wait_ge(st.h, VSTATS_ST)
                    else:
                        sy.wait_ge(sm.h, mm_sm[step - 4])
                    sy.dma_start(out=kvst[:, sl4 * 512:(sl4 + 1) * 512],
                                 in_=v_dram.ap()[ti2 * 128:(ti2 + 1) * 128, dg2 * 512:(dg2 + 1) * 512]
                                 ).then_inc(st.h, 16)
                st.n += 16
                ld_st = st.n

                @blk.scalar
                def _(s, pb=pb, tg=tg, d=d, bank=bank, sl4=sl4, step=step, ld_st=ld_st):
                    s.wait_ge(sm.h, sm.n)
                    s.activation(qraw[:, d * NQ + tg * 512:d * NQ + (tg + 1) * 512],
                                 pb[:, :], AF.Copy,
                                 accum_out=qp_s[:, d * 4 + tg:d * 4 + tg + 1])
                    ii = s.activation(sqdump[:, 0:512], pb[:, :], AF.Square,
                                      accum_out=qp_q[:, d * 4 + tg:d * 4 + tg + 1])
                    se.inc(ii)
                    bgate.release(bank)
                    s.wait_ge(st.h, ld_st)
                    ii = s.square(vsq[:, sl4 * 512:(sl4 + 1) * 512],
                                  kvst[:, sl4 * 512:(sl4 + 1) * 512])
                    se.inc(ii)
                    sq_se[step] = se.n

            for step in range(28, 32):
                emit_statsmm(step)

            @blk.vector
            def _(v):
                v.wait_ge(sm.h, sm.n)
                v.wait_ge(se.h, se.n)
                for d in range(8):
                    v.tensor_reduce(smx[:, d:d + 1], qp_s[:, d * 4:(d + 1) * 4], axis=AXX, op=ALU.add)
                    v.tensor_reduce(smx[:, 24 + d:25 + d], qp_q[:, d * 4:(d + 1) * 4], axis=AXX, op=ALU.add)
                    v.tensor_reduce(smx[:, 8 + d:9 + d], kp_s[:, d * 4:(d + 1) * 4], axis=AXX, op=ALU.add)
                    v.tensor_reduce(smx[:, 32 + d:33 + d], kp_q[:, d * 4:(d + 1) * 4], axis=AXX, op=ALU.add)
                v.tensor_copy(smx[:, 16:24], psv[:, 0:8])
                ii = v.tensor_copy(smx[:, 40:48], psv[:, 8:16])
                sv.inc(ii)

        @blk.scalar
        def _(s):
            s.wait_ge(sv.h, sv.n)
            s.dma_start(out=ccin.ap(), in_=smx[:, 0:48]).then_inc(ss.h, 16)
        ss.n += 16

        @blk.gpsimd
        def _(g):
            g.wait_ge(ss.h, ss.n)
            ii = g.collective_compute("AllReduce", ALU.add,
                                      replica_groups=[list(range(NCORE))],
                                      ins=[ccin.ap()], outs=[ccout.ap()])
            scc.inc(ii)

        @blk.scalar
        def _(s):
            s.wait_ge(scc.h, scc.n)
            s.dma_start(out=stat48[:, :], in_=ccout.ap()).then_inc(ss.h, 16)
        ss.n += 16

        @blk.vector
        def _(v):
            v.wait_ge(ss.h, ss.n)
            ii = v.tensor_scalar(smx[:, 0:24], stat48[:, 0:24], INV16384, None, op0=ALU.mult)
            sv.inc(ii)
            v.wait_ge(sv.h, sv.n)
            ii = v.tensor_scalar(smx[:, 24:48], stat48[:, 24:48], INV16384, None, op0=ALU.mult)
            sv.inc(ii)
            v.wait_ge(sv.h, sv.n)
            ii = v.tensor_tensor(stat48[:, 0:24], smx[:, 0:24], smx[:, 0:24], op=ALU.mult)
            sv.inc(ii)
            v.wait_ge(sv.h, sv.n)
            ii = v.tensor_tensor(stat48[:, 24:48], smx[:, 24:48], stat48[:, 0:24], op=ALU.subtract)
            sv.inc(ii)
            v.wait_ge(sv.h, sv.n)
            ii = v.tensor_scalar(stat48[:, 24:48], stat48[:, 24:48], EPS, None, op0=ALU.add)
            sv.inc(ii)

        @blk.scalar
        def _(s):
            s.wait_ge(sv.h, sv.n)
            ii = s.activation(stat48[:, 0:24], stat48[:, 24:48], AF.Sqrt)
            se.inc(ii)

        @blk.vector
        def _(v):
            v.wait_ge(se.h, se.n)
            ii = v.reciprocal(stat48[:, 24:48], stat48[:, 0:24])
            sv.inc(ii)
            v.wait_ge(sv.h, sv.n)
            ii = v.tensor_tensor(stA[:, :], gamma[:, :], stat48[:, 24:48], op=ALU.mult)
            sv.inc(ii)
            v.wait_ge(sv.h, sv.n)
            ii = v.tensor_tensor(smx[:, 0:24], smx[:, 0:24], stA[:, :], op=ALU.mult)
            sv.inc(ii)
            v.wait_ge(sv.h, sv.n)
            ii = v.tensor_tensor(stB[:, :], beta[:, :], smx[:, 0:24], op=ALU.subtract)
            sv.inc(ii)
            v.wait_ge(sv.h, sv.n)
            ii = v.tensor_tensor(aqk[:, :], stA[:, 0:8], stA[:, 8:16], op=ALU.mult)
            sv.inc(ii)
            v.wait_ge(sv.h, sv.n)
            ii = v.tensor_tensor(bqk[:, :], stB[:, 0:8], stA[:, 8:16], op=ALU.mult)
            sv.inc(ii)
            v.wait_ge(sv.h, sv.n)
            for cch in range(8):
                ii = v.tensor_scalar(qraw[:, cch * NQ:(cch + 1) * NQ],
                                     qraw[:, cch * NQ:(cch + 1) * NQ],
                                     aqk[:, cch:cch + 1], bqk[:, cch:cch + 1],
                                     op0=ALU.mult, op1=ALU.add)
                sv.inc(ii)
        QA_V = sv.n
        KV_STORED = st.n
        P1_SM = sm.n
        P1_SE = se.n

        pA.close()

        pB = qstack.enter_context(contextlib.ExitStack())
        kidxp = pB.enter_context(nc.sbuf_tensor("kidxp", [128, TOK], F32))
        kbuf = pB.enter_context(nc.sbuf_tensor("kbuf", [128, 2 * 4096], BF16))
        masks = pB.enter_context(nc.sbuf_tensor("masks", [128, 4 * TOK], BF16))
        strips = pB.enter_context(nc.sbuf_tensor("strips", [128, 4 * TOK], F32))
        pstr = pB.enter_context(nc.sbuf_tensor("pstr", [128, 2 * TOK], BF16))
        pnrm = pB.enter_context(nc.sbuf_tensor("pnrm", [128, 2 * TOK], BF16))

        @blk.sync
        def _(sy):
            sy.wait_ge(st.h, KV_STORED)
            sy.wait_ge(sm.h, P1_SM)
            sy.wait_ge(se.h, P1_SE)
            sy.wait_ge(sv.h, QA_V)
            sy.dma_start(out=kidxp[:, :], in_=kidx_in.ap()).then_inc(st.h, 16)
        st.n += 16
        KIDX_L = st.n

        with (nc.psum_tensor("sp0", [128, 512], F32) as sp0,
              nc.psum_tensor("sp1", [128, 512], F32) as sp1,
              nc.psum_tensor("sp2", [128, 512], F32) as sp2,
              nc.psum_tensor("sp3", [128, 512], F32) as sp3,
              nc.psum_tensor("sp4", [128, 512], F32) as sp4,
              nc.psum_tensor("sp5", [128, 512], F32) as sp5,
              nc.psum_tensor("sp6", [128, 512], F32) as sp6,
              nc.psum_tensor("sp7", [128, 512], F32) as sp7):
            spb = [sp0, sp1, sp2, sp3, sp4, sp5, sp6, sp7]
            sgate = Gate(sv, 8)
            kgate = Gate(sm, 2)

            def emit_masks(qg):
                @blk.vector
                def _(v, qg=qg):
                    if qg == 0:
                        v.wait_ge(st.h, KIDX_L)
                        v.wait_ge(sv.h, QA_V)
                    for tt in range(4):
                        tglob = qg * 4 + tt
                        ii = v.tensor_scalar(masks[:, tt * TOK:(tt + 1) * TOK], kidxp[:, :],
                                             qpos[:, tglob:tglob + 1], NEGBIG,
                                             op0=ALU.is_gt, op1=ALU.mult)
                        sv.inc(ii)

            emit_masks(0)
            nrm_sv = [0] * 4
            for qg in range(4):
                for ks in range(8):
                    kb = ks % 2

                    @blk.sync
                    def _(sy, qg=qg, ks=ks, kb=kb):
                        if qg == 0 and ks == 0:
                            sy.wait_ge(st.h, KV_STORED)
                        kgate.wait(sy, kb)
                        sy.dma_start(
                            kbuf.ap()[:, kb * 4096:(kb + 1) * 4096].rearrange("p (n k) -> p n k", k=512),
                            k_dram.ap()[:, ks * 512:(ks + 1) * 512].rearrange("(n p) k -> p n k", p=128),
                        ).then_inc(st.h, 16)
                    st.n += 16

                    for tt in range(4):
                        bank = kb * 4 + tt
                        pb = spb[bank]

                        @blk.tensor
                        def _(t, pb=pb, qg=qg, ks=ks, tt=tt, kb=kb, bank=bank):
                            if qg == 0 and ks == 0 and tt == 0:
                                t.wait_ge(sv.h, QA_V)
                            t.wait_ge(st.h, st.n)
                            sgate.wait(t, bank)
                            tglob = qg * 4 + tt
                            for cch in range(8):
                                ii = t.matmul(pb[:, :],
                                              qraw[:, cch * NQ + tglob * 128:cch * NQ + (tglob + 1) * 128],
                                              kbuf[:, kb * 4096 + cch * 512:kb * 4096 + (cch + 1) * 512],
                                              start=(cch == 0), stop=(cch == 7))
                            sm.inc(ii)
                            if tt == 3:
                                kgate.release(kb)

                        @blk.vector
                        def _(v, pb=pb, ks=ks, tt=tt, bank=bank):
                            v.wait_ge(sm.h, sm.n)
                            ii = v.scalar_tensor_tensor(
                                strips[:, tt * TOK + ks * 512:tt * TOK + (ks + 1) * 512],
                                pb[:, :], SMSCALE,
                                masks[:, tt * TOK + ks * 512:tt * TOK + (ks + 1) * 512],
                                op0=ALU.mult, op1=ALU.add)
                            sv.inc(ii)
                            sgate.release(bank)

                red_sv = [0] * 4
                for tt in range(4):
                    @blk.vector
                    def _(v, tt=tt):
                        v.wait_ge(sv.h, sv.n)
                        ii = v.tensor_reduce(smx[:, 48 + tt:49 + tt], strips[:, tt * TOK:(tt + 1) * TOK],
                                             axis=AXX, op=ALU.max, negate=True)
                        sv.inc(ii)
                        red_sv[tt] = sv.n

                if qg < 3:
                    emit_masks(qg + 1)

                for tt in range(4):
                    @blk.scalar
                    def _(s, tt=tt):
                        s.wait_ge(sv.h, red_sv[tt])
                        s.wait_ge(sv.h, nrm_sv[(tt + 2) % 4])
                        ii = s.activation(pstr[:, (tt % 2) * TOK:(tt % 2 + 1) * TOK],
                                          strips[:, tt * TOK:(tt + 1) * TOK], AF.Exp,
                                          bias=smx[:, 48 + tt:49 + tt], scale=1.0,
                                          accum_out=smx[:, 52 + tt:53 + tt])
                        se.inc(ii)

                    @blk.vector
                    def _(v, tt=tt):
                        v.wait_ge(se.h, se.n)
                        v.wait_ge(st.h, st.n)
                        ii = v.reciprocal(smx[:, 56 + tt:57 + tt], smx[:, 52 + tt:53 + tt])
                        sv.inc(ii)
                        v.wait_ge(sv.h, sv.n)
                        ii = v.tensor_scalar(pnrm[:, (tt % 2) * TOK:(tt % 2 + 1) * TOK],
                                             pstr[:, (tt % 2) * TOK:(tt % 2 + 1) * TOK],
                                             smx[:, 56 + tt:57 + tt], None, op0=ALU.mult)
                        sv.inc(ii)
                        nrm_sv[tt] = sv.n

                    @blk.sync
                    def _(sy, qg=qg, tt=tt):
                        sy.wait_ge(sv.h, sv.n)
                        tglob = qg * 4 + tt
                        sy.dma_start(out=p_dram.ap()[tglob * 128:(tglob + 1) * 128, :],
                                     in_=pnrm[:, (tt % 2) * TOK:(tt % 2 + 1) * TOK]).then_inc(st.h, 16)
                    st.n += 16
        P_STORED = st.n
        P2_SM_END = sm.n
        pB.close()
        qstack.close()

        p3 = ctx.enter_context(contextlib.ExitStack())
        wpt = p3.enter_context(nc.sbuf_tensor("wpt", [128, 8 * C], BF16))
        wmpt = p3.enter_context(nc.sbuf_tensor("wmpt", [128, 8 * C], BF16))
        apt = p3.enter_context(nc.sbuf_tensor("apt", [128, C], BF16))
        lpbs = p3.enter_context(nc.sbuf_tensor("lpbs", [64, C], BF16))
        ptbuf = p3.enter_context(nc.sbuf_tensor("ptbuf", [128, 32 * 1024], BF16))
        vbuf = p3.enter_context(nc.sbuf_tensor("vbuf", [128, 3 * 512], BF16))
        aout = p3.enter_context(nc.sbuf_tensor("aout", [128, 8 * 1024], BF16))
        y1 = p3.enter_context(nc.sbuf_tensor("y1", [128, 8 * 1024], BF16))
        vres = p3.enter_context(nc.sbuf_tensor("vres", [128, 2 * 1024], F32))
        xtile = p3.enter_context(nc.sbuf_tensor("xtile", [128, 2 * 1024], F32))
        x2t = p3.enter_context(nc.sbuf_tensor("x2t", [128, 2 * 1024], F32))
        q_u = p3.enter_context(nc.sbuf_tensor("q_u", [128, 1024], F32))
        q_qg = p3.enter_context(nc.sbuf_tensor("q_qg", [128, 1024], BF16))
        q_t0 = p3.enter_context(nc.sbuf_tensor("q_t0", [128, 1024], BF16))
        q_sg = p3.enter_context(nc.sbuf_tensor("q_sg", [128, 1024], BF16))
        q_bm = p3.enter_context(nc.sbuf_tensor("q_bm", [128, 64], F32))
        q_sc = p3.enter_context(nc.sbuf_tensor("q_sc", [128, 64], F32))
        q_rs = p3.enter_context(nc.sbuf_tensor("q_rs", [128, 64], F32))
        q_out = p3.enter_context(nc.sbuf_tensor("q_out", [128, 2 * 1024], BF16))
        lnt = p3.enter_context(nc.sbuf_tensor("lnt", [128, 16], F32))

        @blk.sync
        def _(sy):
            sy.wait_ge(sm.h, P2_SM_END)
            sy.wait_ge(st.h, P_STORED)
            for c in range(8):
                sy.dma_start(wpt[:, c * C:(c + 1) * C],
                             wpb.ap()[0:C, c * 128:(c + 1) * 128],
                             transpose=True).then_inc(st.h, 16)
            sy.dma_start(apt[:, :], lpa_b.ap()[0:C, 0:128], transpose=True).then_inc(st.h, 16)
            sy.dma_start(lpbs[:, :], lpb_b.ap()).then_inc(st.h, 16)
        st.n += 10 * 16
        WP_L = st.n

        with (nc.psum_tensor("psp0", [128, 512], F32) as psp0,
              nc.psum_tensor("psp1", [128, 512], F32) as psp1):
            psp = [psp0, psp1]
            pgate = Gate(sv, 2)
            for idx in range(16):
                jc, hf = idx // 2, idx % 2
                slot = idx % 2
                pb = psp[slot]

                @blk.tensor
                def _(t, pb=pb, idx=idx, jc=jc, hf=hf, slot=slot):
                    if idx == 0:
                        t.wait_ge(st.h, WP_L)
                    pgate.wait(t, slot)
                    ii = t.matmul(pb[:, :], lpbs[0:64, jc * 128:(jc + 1) * 128],
                                  apt[0:64, hf * 512:(hf + 1) * 512],
                                  start=True, stop=True)
                    sm.inc(ii)

                @blk.vector
                def _(v, pb=pb, jc=jc, hf=hf, slot=slot):
                    v.wait_ge(sm.h, sm.n)
                    ii = v.tensor_tensor(
                        wmpt[:, jc * C + hf * 512:jc * C + (hf + 1) * 512],
                        wpt[:, jc * C + hf * 512:jc * C + (hf + 1) * 512],
                        pb[:, :], op=ALU.add)
                    sv.inc(ii)
                    pgate.release(slot)
        WMPT_V = sv.n

        with (nc.psum_tensor("pv0", [128, 512], F32) as pv0,
              nc.psum_tensor("pv1", [128, 512], F32) as pv1,
              nc.psum_tensor("pv2", [128, 512], F32) as pv2,
              nc.psum_tensor("pv3", [128, 512], F32) as pv3,
              nc.psum_tensor("pv4", [128, 512], F32) as pv4,
              nc.psum_tensor("pv5", [128, 512], F32) as pv5,
              nc.psum_tensor("pv6", [128, 512], F32) as pv6,
              nc.psum_tensor("pv7", [128, 512], F32) as pv7):
            pvb = [pv0, pv1, pv2, pv3, pv4, pv5, pv6, pv7]
            vslotg = Gate(sm, 3)
            pvg = Gate(sv, 8)
            p1g = Gate(se, 8)
            p2g = Gate(sv, 8)
            xtg = Gate(sv, 2)
            x2g = Gate(ss, 2)

            PV_SM_END = [0]
            for half in range(2):
                HALF_SV = sv.n
                HALF_SE = se.n

                @blk.sync
                def _(sy, half=half):
                    if half == 0:
                        sy.wait_ge(st.h, P_STORED)
                    else:
                        sy.wait_ge(sm.h, PV_SM_END[0])
                    for kc in range(32):
                        sy.dma_start(ptbuf[:, kc * 1024:(kc + 1) * 1024],
                                     p_dram.ap()[half * 1024:(half + 1) * 1024, kc * 128:(kc + 1) * 128],
                                     transpose=True).then_inc(st.h, 16)
                st.n += 32 * 16

                for cg in range(2):
                    for kc in range(32):
                        vslot = kc % 3

                        @blk.sync
                        def _(sy, half=half, cg=cg, kc=kc, vslot=vslot):
                            if half == 0 and cg == 0 and kc == 0:
                                sy.wait_ge(st.h, KV_STORED)
                            vslotg.wait(sy, vslot)
                            sy.dma_start(vbuf[:, vslot * 512:(vslot + 1) * 512],
                                         v_dram.ap()[kc * 128:(kc + 1) * 128, cg * 512:(cg + 1) * 512]
                                         ).then_inc(st.h, 16)
                        st.n += 16

                        @blk.tensor
                        def _(t, cg=cg, kc=kc, vslot=vslot, HALF_SV=HALF_SV, HALF_SE=HALF_SE):
                            t.wait_ge(st.h, st.n)
                            if kc == 0:
                                t.wait_ge(sv.h, HALF_SV)
                                t.wait_ge(se.h, HALF_SE)
                                for bb in range(8):
                                    pvg.wait(t, bb)
                            for cc in range(4):
                                for qq in range(2):
                                    ii = t.matmul(pvb[cc * 2 + qq][:, :],
                                                  vbuf[:, vslot * 512 + cc * 128:vslot * 512 + (cc + 1) * 128],
                                                  ptbuf[:, kc * 1024 + qq * 512:kc * 1024 + (qq + 1) * 512],
                                                  start=(kc == 0), stop=(kc == 31))
                            sm.inc(ii)
                            vslotg.release(vslot)

                    @blk.vector
                    def _(v, cg=cg):
                        v.wait_ge(sm.h, sm.n)
                        for cc in range(4):
                            cglob = cg * 4 + cc
                            for qq in range(2):
                                ii = v.tensor_scalar(
                                    aout[:, cglob * 1024 + qq * 512:cglob * 1024 + (qq + 1) * 512],
                                    pvb[cc * 2 + qq][:, :],
                                    stA[:, 16 + cglob:17 + cglob], stB[:, 16 + cglob:17 + cglob],
                                    op0=ALU.mult, op1=ALU.add)
                        sv.inc(ii)
                        for bb in range(8):
                            pvg.release(bb)
                AOUT_V = sv.n
                PV_SM_END[0] = sm.n

                for cp in range(8):
                    b0, b1 = (cp * 2) % 8, (cp * 2 + 1) % 8

                    @blk.tensor
                    def _(t, cp=cp, b0=b0, b1=b1, AOUT_V=AOUT_V):
                        if cp == 0:
                            t.wait_ge(sv.h, AOUT_V)
                            t.wait_ge(st.h, WP_L)
                        p1g.wait(t, b0)
                        p1g.wait(t, b1)
                        for qq in range(2):
                            pb = pvb[b0 if qq == 0 else b1]
                            for cc in range(8):
                                ii = t.matmul(pb[:, :],
                                              wpt[:, cc * C + cp * 128:cc * C + (cp + 1) * 128],
                                              aout[:, cc * 1024 + qq * 512:cc * 1024 + (qq + 1) * 512],
                                              start=(cc == 0), stop=(cc == 7))
                        sm.inc(ii)

                    @blk.scalar
                    def _(s, cp=cp, b0=b0, b1=b1):
                        s.wait_ge(sm.h, sm.n)
                        s.activation(y1[:, cp * 1024:cp * 1024 + 512], pvb[b0][:, :], AF.Copy)
                        ii = s.activation(y1[:, cp * 1024 + 512:(cp + 1) * 1024], pvb[b1][:, :], AF.Copy)
                        se.inc(ii)
                        p1g.release(b0)
                        p1g.release(b1)
                Y1_S = se.n

                for tt in range(8):
                    gt = half * 8 + tt
                    sl = tt % 2
                    b0, b1 = (tt * 2) % 8, (tt * 2 + 1) % 8

                    @blk.scalar
                    def _(s, gt=gt, sl=sl, half=half, tt=tt):
                        if half == 0 and tt == 0:
                            s.wait_ge(st.h, P_STORED)
                        xtg.wait(s, sl)
                        s.dma_start(out=xtile[:, sl * 1024:(sl + 1) * 1024],
                                    in_=xin.ap()[gt * 128:(gt + 1) * 128, :]).then_inc(ss.h, 16)
                    ss.n += 16
                    XT_L = ss.n

                    @blk.tensor
                    def _(t, tt=tt, b0=b0, b1=b1, Y1_S=Y1_S):
                        if tt == 0:
                            t.wait_ge(se.h, Y1_S)
                        p2g.wait(t, b0)
                        p2g.wait(t, b1)
                        for dg in range(2):
                            pb = pvb[b0 if dg == 0 else b1]
                            for cc in range(8):
                                ii = t.matmul(pb[:, :],
                                              y1[:, cc * 1024 + tt * 128:cc * 1024 + (tt + 1) * 128],
                                              wmpt[:, cc * C + dg * 512:cc * C + (dg + 1) * 512],
                                              start=(cc == 0), stop=(cc == 7))
                        sm.inc(ii)

                    @blk.vector
                    def _(v, sl=sl, b0=b0, b1=b1, XT_L=XT_L):
                        v.wait_ge(sm.h, sm.n)
                        v.wait_ge(ss.h, XT_L)
                        x2g.wait(v, sl)
                        v.tensor_tensor(vres[:, sl * 1024:sl * 1024 + 512], pvb[b0][:, :],
                                        xtile[:, sl * 1024:sl * 1024 + 512], op=ALU.add)
                        ii = v.tensor_tensor(vres[:, sl * 1024 + 512:(sl + 1) * 1024], pvb[b1][:, :],
                                             xtile[:, sl * 1024 + 512:(sl + 1) * 1024], op=ALU.add)
                        sv.inc(ii)
                        p2g.release(b0)
                        p2g.release(b1)
                        xtg.release(sl)

                    emit_ln(blk, sv, se, vres, x2t, lnt, sl, sqdump)

                    @blk.scalar
                    def _(s, gt=gt, sl=sl):
                        s.wait_ge(sv.h, sv.n)
                        s.dma_start(out=x2_dram.ap()[gt * 128:(gt + 1) * 128, :],
                                    in_=x2t[:, sl * 1024:(sl + 1) * 1024]).then_inc(ss.h, 16)
                    ss.n += 16

                    emit_quant(blk, sv, se, x2t, sl * 1024, q_u, q_qg, q_t0, q_sg,
                               q_bm, q_sc, q_rs, q_out, sl * 1024, 1024)

                    @blk.scalar
                    def _(s, gt=gt, sl=sl):
                        s.wait_ge(sv.h, sv.n)
                        s.dma_start(out=xq_dram.ap()[gt * 128:(gt + 1) * 128, :],
                                    in_=q_out[:, sl * 1024:(sl + 1) * 1024]).then_inc(ss.h, 16)
                    ss.n += 16
                    x2g.release(sl)
        XQ_STORED = ss.n
        p3.close()

        p4 = ctx.enter_context(contextlib.ExitStack())
        xqfm = p4.enter_context(nc.sbuf_tensor("xqfm", [128, 8 * NQ], BF16))
        wgt = p4.enter_context(nc.sbuf_tensor("wgt", [128, 8 * I], BF16))
        wut = p4.enter_context(nc.sbuf_tensor("wut", [128, 8 * I], BF16))
        silu = p4.enter_context(nc.sbuf_tensor("silu", [128, 512], F32))
        htile = p4.enter_context(nc.sbuf_tensor("htile", [128, 2 * I], F32))
        h_u = p4.enter_context(nc.sbuf_tensor("h_u", [128, I], F32))
        h_qg = p4.enter_context(nc.sbuf_tensor("h_qg", [128, I], BF16))
        h_t0 = p4.enter_context(nc.sbuf_tensor("h_t0", [128, I], BF16))
        h_sg = p4.enter_context(nc.sbuf_tensor("h_sg", [128, I], BF16))
        h_bm = p4.enter_context(nc.sbuf_tensor("h_bm", [128, I // 16], F32))
        h_sc = p4.enter_context(nc.sbuf_tensor("h_sc", [128, I // 16], F32))
        h_rs = p4.enter_context(nc.sbuf_tensor("h_rs", [128, I // 16], F32))
        h_out = p4.enter_context(nc.sbuf_tensor("h_out", [128, 2 * I], BF16))

        @blk.sync
        def _(sy):
            sy.wait_ge(ss.h, XQ_STORED)
            sy.wait_ge(scc.h, AG_DONE)
            for c in range(8):
                sy.dma_start(xqfm[:, c * NQ:(c + 1) * NQ],
                             xq_dram.ap()[0:NQ, c * 128:(c + 1) * 128],
                             transpose=True).then_inc(st.h, 16)
            for c in range(8):
                sy.dma_start(wgt[:, c * I:(c + 1) * I],
                             wgq.ap()[0:I, c * 128:(c + 1) * 128],
                             transpose=True).then_inc(st.h, 16)
            for c in range(8):
                sy.dma_start(wut[:, c * I:(c + 1) * I],
                             wuq.ap()[0:I, c * 128:(c + 1) * 128],
                             transpose=True).then_inc(st.h, 16)
        st.n += 24 * 16
        FFNW_L = st.n

        IGRP = [(0, 512), (512, 512), (1024, 512), (1536, 512), (2048, 512), (2560, 256)]
        with (nc.psum_tensor("fg0", [128, 512], F32) as fg0,
              nc.psum_tensor("fg1", [128, 512], F32) as fg1,
              nc.psum_tensor("fu0", [128, 512], F32) as fu0,
              nc.psum_tensor("fu1", [128, 512], F32) as fu1):
            fgb, fub = [fg0, fg1], [fu0, fu1]
            fgate = Gate(sv, 2)
            hqg = Gate(ss, 2)

            for tile in range(16):
                sl = tile % 2
                for gi, (ib, iw) in enumerate(IGRP):
                    pb = gi % 2

                    @blk.tensor
                    def _(t, tile=tile, gi=gi, ib=ib, iw=iw, pb=pb):
                        if tile == 0 and gi == 0:
                            t.wait_ge(st.h, FFNW_L - 8 * 16)
                        fgate.wait(t, pb)
                        for cc in range(8):
                            ii = t.matmul(fgb[pb][:, 0:iw],
                                          xqfm[:, cc * NQ + tile * 128:cc * NQ + (tile + 1) * 128],
                                          wgt[:, cc * I + ib:cc * I + ib + iw],
                                          start=(cc == 0), stop=(cc == 7))
                        sm.inc(ii)
                        if tile == 0 and gi == 0:
                            t.wait_ge(st.h, FFNW_L)
                        for cc in range(8):
                            ii = t.matmul(fub[pb][:, 0:iw],
                                          xqfm[:, cc * NQ + tile * 128:cc * NQ + (tile + 1) * 128],
                                          wut[:, cc * I + ib:cc * I + ib + iw],
                                          start=(cc == 0), stop=(cc == 7))
                        sm.inc(ii)

                    @blk.scalar
                    def _(s, gi=gi, pb=pb, iw=iw):
                        s.wait_ge(sm.h, sm.n)
                        ii = s.activation(silu[:, 0:iw], fgb[pb][:, 0:iw], AF.Silu)
                        se.inc(ii)

                    @blk.vector
                    def _(v, sl=sl, ib=ib, iw=iw, pb=pb, gi=gi):
                        v.wait_ge(se.h, se.n)
                        if gi == 0:
                            hqg.wait(v, sl)
                        ii = v.tensor_tensor(htile[:, sl * I + ib:sl * I + ib + iw],
                                             silu[:, 0:iw], fub[pb][:, 0:iw], op=ALU.mult)
                        sv.inc(ii)
                        fgate.release(pb)

                emit_quant(blk, sv, se, htile, sl * I, h_u, h_qg, h_t0, h_sg,
                           h_bm, h_sc, h_rs, h_out, sl * I, I)

                @blk.scalar
                def _(s, tile=tile, sl=sl):
                    s.wait_ge(sv.h, sv.n)
                    s.dma_start(out=hq_dram.ap()[tile * 128:(tile + 1) * 128, :],
                                in_=h_out[:, sl * I:(sl + 1) * I]).then_inc(ss.h, 16)
                ss.n += 16
                hqg.release(sl)
        HQ_STORED = ss.n
        p4.close()

        p5 = ctx.enter_context(contextlib.ExitStack())
        hqfm = p5.enter_context(nc.sbuf_tensor("hqfm", [128, 22 * NQ], BF16))
        wdt = p5.enter_context(nc.sbuf_tensor("wdt", [128, 22 * C], BF16))
        x2tile = p5.enter_context(nc.sbuf_tensor("x2tile", [128, 2 * C], F32))
        v2res = p5.enter_context(nc.sbuf_tensor("v2res", [128, 2 * C], F32))
        otile = p5.enter_context(nc.sbuf_tensor("otile", [128, 2 * C], F32))
        ln2 = p5.enter_context(nc.sbuf_tensor("ln2", [128, 16], F32))

        @blk.sync
        def _(sy):
            sy.wait_ge(ss.h, HQ_STORED)
            for c in range(22):
                sy.dma_start(hqfm[:, c * NQ:(c + 1) * NQ],
                             hq_dram.ap()[0:NQ, c * 128:(c + 1) * 128],
                             transpose=True).then_inc(st.h, 16)
            for c in range(22):
                sy.dma_start(wdt[:, c * C:(c + 1) * C],
                             wdq.ap()[0:C, c * 128:(c + 1) * 128],
                             transpose=True).then_inc(st.h, 16)
        st.n += 44 * 16
        P5_L = st.n

        with (nc.psum_tensor("dn0", [128, 512], F32) as dn0,
              nc.psum_tensor("dn1", [128, 512], F32) as dn1,
              nc.psum_tensor("dn2", [128, 512], F32) as dn2,
              nc.psum_tensor("dn3", [128, 512], F32) as dn3):
            dnb = [dn0, dn1, dn2, dn3]
            dgate = Gate(sv, 4)
            xslot = Gate(sv, 2)
            ygate = Gate(so, 2)

            for tile in range(16):
                sl = tile % 2
                b0, b1 = sl * 2, sl * 2 + 1

                @blk.scalar
                def _(s, tile=tile, sl=sl):
                    xslot.wait(s, sl)
                    s.dma_start(out=x2tile[:, sl * C:(sl + 1) * C],
                                in_=x2_dram.ap()[tile * 128:(tile + 1) * 128, :]).then_inc(ss.h, 16)
                ss.n += 16
                X2T_L = ss.n

                @blk.tensor
                def _(t, tile=tile, b0=b0, b1=b1):
                    if tile == 0:
                        t.wait_ge(st.h, P5_L)
                    dgate.wait(t, b0)
                    dgate.wait(t, b1)
                    for dg in range(2):
                        pb = dnb[b0 if dg == 0 else b1]
                        for cc in range(22):
                            ii = t.matmul(pb[:, :],
                                          hqfm[:, cc * NQ + tile * 128:cc * NQ + (tile + 1) * 128],
                                          wdt[:, cc * C + dg * 512:cc * C + (dg + 1) * 512],
                                          start=(cc == 0), stop=(cc == 21))
                    sm.inc(ii)

                @blk.vector
                def _(v, sl=sl, b0=b0, b1=b1, X2T_L=X2T_L):
                    v.wait_ge(sm.h, sm.n)
                    v.wait_ge(ss.h, X2T_L)
                    ygate.wait(v, sl)
                    v.tensor_tensor(v2res[:, sl * C:sl * C + 512], dnb[b0][:, :],
                                    x2tile[:, sl * C:sl * C + 512], op=ALU.add)
                    ii = v.tensor_tensor(v2res[:, sl * C + 512:(sl + 1) * C], dnb[b1][:, :],
                                         x2tile[:, sl * C + 512:(sl + 1) * C], op=ALU.add)
                    sv.inc(ii)
                    dgate.release(b0)
                    dgate.release(b1)
                    xslot.release(sl)

                emit_ln(blk, sv, se, v2res, otile, ln2, sl, sqdump)

                @blk.scalar
                def _(s, tile=tile, sl=sl):
                    s.wait_ge(sv.h, sv.n)
                    s.dma_start(out=yout.ap()[tile * 128:(tile + 1) * 128, :],
                                in_=otile[:, sl * C:(sl + 1) * C]).then_inc(so.h, 16)
                so.n += 16
                ygate.release(sl)

        @blk.scalar
        def _(s):
            s.wait_ge(so.h, so.n)

    return nc


_CACHE = {}


def kernel(L0_attn_x, L0_attn_W_attn, L0_attn_W_proj, L0_attn_lora_attn_A,
           L0_attn_lora_attn_B, L0_attn_lora_proj_A, L0_attn_lora_proj_B,
           L0_attn_bn_gamma, L0_attn_bn_beta, L0_ffn_w_gate, L0_ffn_w_up,
           L0_ffn_w_down):
    x = np.ascontiguousarray(np.asarray(L0_attn_x, dtype=np.float32))
    wa = np.ascontiguousarray(np.asarray(L0_attn_W_attn, dtype=np.float32))
    wp = np.ascontiguousarray(np.asarray(L0_attn_W_proj, dtype=np.float32))
    A1 = np.ascontiguousarray(np.asarray(L0_attn_lora_attn_A, dtype=np.float32))
    B1 = np.ascontiguousarray(np.asarray(L0_attn_lora_attn_B, dtype=np.float32))
    A2 = np.ascontiguousarray(np.asarray(L0_attn_lora_proj_A, dtype=np.float32))
    B2 = np.ascontiguousarray(np.asarray(L0_attn_lora_proj_B, dtype=np.float32))
    gam = np.asarray(L0_attn_bn_gamma, dtype=np.float32)
    bet = np.asarray(L0_attn_bn_beta, dtype=np.float32)
    wg = np.ascontiguousarray(np.asarray(L0_ffn_w_gate, dtype=np.float32))
    wu = np.ascontiguousarray(np.asarray(L0_ffn_w_up, dtype=np.float32))
    wd = np.ascontiguousarray(np.asarray(L0_ffn_w_down, dtype=np.float32))

    if "nc" not in _CACHE:
        _CACHE["nc"] = build_nc()
    nc = _CACHE["nc"]

    gam2d = np.ascontiguousarray(gam.reshape(24, 128).T)
    bet2d = np.ascontiguousarray(bet.reshape(24, 128).T)
    ones = np.ones((128, 1), dtype=np.float32)
    shsz = 128 * SHF

    in_maps = []
    for c in range(NCORE):
        s, h = c % 4, c // 4
        own = x[s, h * NQ:(h + 1) * NQ, :]
        sib = x[s, (1 - h) * NQ:(2 - h) * NQ, :]
        xp = np.ascontiguousarray(np.concatenate([own, sib], axis=0))
        kidx_row = np.concatenate([np.arange(h * NQ, (h + 1) * NQ),
                                   np.arange((1 - h) * NQ, (2 - h) * NQ)]).astype(np.float32)
        kidxp = np.ascontiguousarray(np.broadcast_to(kidx_row, (128, TOK)))
        qp = (h * NQ + (np.arange(16) * 128)[None, :] + np.arange(128)[:, None]).astype(np.float32)
        in_maps.append({
            "x": xp,
            "w_attn": wa, "w_proj": wp,
            "laa": A1, "lab": B1, "lpa": A2, "lpb": B2,
            "gam2d": gam2d, "bet2d": bet2d,
            "wg_sh": np.ascontiguousarray(wg.reshape(-1)[c * shsz:(c + 1) * shsz].reshape(128, SHF)),
            "wu_sh": np.ascontiguousarray(wu.reshape(-1)[c * shsz:(c + 1) * shsz].reshape(128, SHF)),
            "wd_sh": np.ascontiguousarray(wd.reshape(-1)[c * shsz:(c + 1) * shsz].reshape(128, SHF)),
            "kidx": kidxp, "qpos": np.ascontiguousarray(qp), "ones": ones,
        })

    res = run_bass_kernel_spmd(nc, in_maps, core_ids=list(range(NCORE)))

    out = np.empty((B, T, C), dtype=np.float32)
    for c in range(NCORE):
        s, h = c % 4, c // 4
        out[s, h * NQ:(h + 1) * NQ, :] = res.results[c]["y"]
    return out


# version 14
# speedup vs baseline: 1.5877x; 1.0043x over previous
import numpy as np
import concourse.bass as bass
import concourse.mybir as mybir
from concourse.bass_utils import run_bass_kernel_spmd

F32 = mybir.dt.float32
BF16 = mybir.dt.bfloat16
AF = mybir.ActivationFunctionType
ALU = mybir.AluOpType
AXX = mybir.AxisListType.X

B, T, C, D3, R, I = 4, 4096, 1024, 3072, 64, 2816
EPS = 1e-5
TOK = 4096
NQ = 2048
NCORE = 8
SHF = (I * C) // NCORE // 128
SHQ = SHF // 4
NEGBIG = -10000.0
M2 = [0.0625, 0.5625, 1.5625, 3.0625, 6.25, 12.25, 25.0]
WTS = [0.5, 0.5, 0.5, 0.5, 1.0, 1.0, 2.0]
INV1024 = 1.0 / 1024.0
INV16384 = 1.0 / 16384.0
SMSCALE = 0.03125


class Sem:
    __slots__ = ("h", "n")

    def __init__(self, h):
        self.h = h
        self.n = 0

    def inc(self, ins, k=1):
        ins.then_inc(self.h, k)
        self.n += k
        return self.n


class Gate:

    def __init__(self, sem, nslots):
        self.sem = sem
        self.v = [0] * nslots

    def wait(self, eng, slot):
        if self.v[slot] > 0:
            eng.wait_ge(self.sem.h, self.v[slot])

    def release(self, slot):
        self.v[slot] = self.sem.n


def emit_quant(blk, sv, se, src, osrc, u, qg, t0, sgn, bm, sc, rs, out, oout, n):
    nb = n // 16

    @blk.vector
    def _(v):
        i = v.tensor_reduce(bm[:, 0:nb, None],
                            src.ap()[:, osrc:osrc + n].rearrange("p (a b) -> p a b", b=16),
                            axis=AXX, op=ALU.max, apply_absolute_value=True)
        sv.inc(i)
        v.wait_ge(sv.h, sv.n)
        i = v.tensor_scalar(sc[:, 0:nb], bm[:, 0:nb], 1.0 / 6.0, 1e-12,
                            op0=ALU.mult, op1=ALU.max)
        sv.inc(i)
        v.wait_ge(sv.h, sv.n)
        i = v.reciprocal(rs[:, 0:nb], sc[:, 0:nb])
        sv.inc(i)
        v.wait_ge(sv.h, sv.n)
        i = v.tensor_tensor(u.ap()[:, 0:n].rearrange("p (a b) -> p a b", b=16),
                            src.ap()[:, osrc:osrc + n].rearrange("p (a b) -> p a b", b=16),
                            rs[:, 0:nb, None].broadcast_to((128, nb, 16)),
                            op=ALU.mult)
        sv.inc(i)

    @blk.scalar
    def _(s):
        s.wait_ge(sv.h, sv.n)
        i = s.sign(sgn[:, 0:n], u[:, 0:n])
        se.inc(i)
        s.wait_ge(se.h, se.n)
        i = s.square(u[:, 0:n], u[:, 0:n])
        se.inc(i)

    @blk.vector
    def _(v):
        v.wait_ge(se.h, se.n)
        v.tensor_scalar(qg[:, 0:n], u[:, 0:n], M2[0], WTS[0],
                        op0=ALU.is_gt, op1=ALU.mult)
        for j in range(1, 7):
            v.tensor_scalar(t0[:, 0:n], u[:, 0:n], M2[j], WTS[j],
                            op0=ALU.is_gt, op1=ALU.mult)
            v.tensor_tensor(qg[:, 0:n], qg[:, 0:n], t0[:, 0:n], op=ALU.add)
        v.tensor_tensor(qg[:, 0:n], qg[:, 0:n], sgn[:, 0:n], op=ALU.mult)
        i = v.tensor_tensor(out.ap()[:, oout:oout + n].rearrange("p (a b) -> p a b", b=16),
                            qg.ap()[:, 0:n].rearrange("p (a b) -> p a b", b=16),
                            sc[:, 0:nb, None].broadcast_to((128, nb, 16)),
                            op=ALU.mult)
        sv.inc(i)


def emit_ln(blk, sv, se, vres, outt, lnt, sl, sqdump):
    a, b_, c_, d_ = sl * 4, sl * 4 + 1, sl * 4 + 2, sl * 4 + 3
    o = sl * 1024

    @blk.vector
    def _(v):
        i = v.tensor_reduce(lnt[:, a:a + 1], vres[:, o:o + 1024], axis=AXX, op=ALU.add)
        sv.inc(i)

    @blk.scalar
    def _(s):
        s.wait_ge(sv.h, sv.n)
        i = s.activation(sqdump[:, :], vres[:, o:o + 1024], AF.Square,
                         accum_out=lnt[:, b_:b_ + 1])
        se.inc(i)

    @blk.vector
    def _(v):
        v.wait_ge(se.h, se.n)
        i = v.tensor_scalar(lnt[:, a:a + 1], lnt[:, a:a + 1], INV1024, None, op0=ALU.mult)
        sv.inc(i)
        v.wait_ge(sv.h, sv.n)
        i = v.tensor_scalar(lnt[:, b_:b_ + 1], lnt[:, b_:b_ + 1], INV1024, None, op0=ALU.mult)
        sv.inc(i)
        v.wait_ge(sv.h, sv.n)
        i = v.tensor_tensor(lnt[:, c_:c_ + 1], lnt[:, a:a + 1], lnt[:, a:a + 1], op=ALU.mult)
        sv.inc(i)
        v.wait_ge(sv.h, sv.n)
        i = v.tensor_tensor(lnt[:, b_:b_ + 1], lnt[:, b_:b_ + 1], lnt[:, c_:c_ + 1],
                            op=ALU.subtract)
        sv.inc(i)
        v.wait_ge(sv.h, sv.n)
        i = v.tensor_scalar(lnt[:, b_:b_ + 1], lnt[:, b_:b_ + 1], EPS, None, op0=ALU.add)
        sv.inc(i)

    @blk.scalar
    def _(s):
        s.wait_ge(sv.h, sv.n)
        i = s.activation(lnt[:, c_:c_ + 1], lnt[:, b_:b_ + 1], AF.Sqrt)
        se.inc(i)

    @blk.vector
    def _(v):
        v.wait_ge(se.h, se.n)
        i = v.reciprocal(lnt[:, d_:d_ + 1], lnt[:, c_:c_ + 1])
        sv.inc(i)
        v.wait_ge(sv.h, sv.n)
        i = v.tensor_tensor(lnt[:, c_:c_ + 1], lnt[:, a:a + 1], lnt[:, d_:d_ + 1],
                            op=ALU.mult)
        sv.inc(i)
        v.wait_ge(sv.h, sv.n)
        i = v.tensor_scalar(lnt[:, c_:c_ + 1], lnt[:, c_:c_ + 1], -1.0, None, op0=ALU.mult)
        sv.inc(i)
        v.wait_ge(sv.h, sv.n)
        i = v.tensor_scalar(outt[:, o:o + 1024], vres[:, o:o + 1024],
                            lnt[:, d_:d_ + 1], lnt[:, c_:c_ + 1],
                            op0=ALU.mult, op1=ALU.add)
        sv.inc(i)


def build_nc():
    import contextlib
    nc = bass.Bass("TRN2", target_bir_lowering=False)

    xin = nc.dram_tensor("x", [TOK, C], F32, kind="ExternalInput")
    w_attn = nc.dram_tensor("w_attn", [D3, C], F32, kind="ExternalInput")
    w_proj = nc.dram_tensor("w_proj", [C, C], F32, kind="ExternalInput")
    laa = nc.dram_tensor("laa", [C, R], F32, kind="ExternalInput")
    lab = nc.dram_tensor("lab", [R, D3], F32, kind="ExternalInput")
    lpa = nc.dram_tensor("lpa", [C, R], F32, kind="ExternalInput")
    lpb = nc.dram_tensor("lpb", [R, C], F32, kind="ExternalInput")
    gam2d = nc.dram_tensor("gam2d", [128, 24], F32, kind="ExternalInput")
    bet2d = nc.dram_tensor("bet2d", [128, 24], F32, kind="ExternalInput")
    wg_sh = nc.dram_tensor("wg_sh", [128, SHF], F32, kind="ExternalInput")
    wu_sh = nc.dram_tensor("wu_sh", [128, SHF], F32, kind="ExternalInput")
    wd_sh = nc.dram_tensor("wd_sh", [128, SHF], F32, kind="ExternalInput")
    kidx_in = nc.dram_tensor("kidx", [128, TOK], F32, kind="ExternalInput")
    qpos_in = nc.dram_tensor("qpos", [128, 16], F32, kind="ExternalInput")
    ones_in = nc.dram_tensor("ones", [128, 1], F32, kind="ExternalInput")
    yout = nc.dram_tensor("y", [NQ, C], F32, kind="ExternalOutput")

    xp_b = nc.dram_tensor("xp_b", [TOK, C], BF16)
    wab = nc.dram_tensor("wab", [D3, C], BF16)
    wpb = nc.dram_tensor("wpb", [C, C], BF16)
    laa_b = nc.dram_tensor("laa_b", [C, 128], BF16)
    lpa_b = nc.dram_tensor("lpa_b", [C, 128], BF16)
    lab_b = nc.dram_tensor("lab_b", [R, D3], BF16)
    lpb_b = nc.dram_tensor("lpb_b", [R, C], BF16)
    k_dram = nc.dram_tensor("k_dram", [C, TOK], BF16)
    v_dram = nc.dram_tensor("v_dram", [TOK, C], BF16)
    p_dram = nc.dram_tensor("p_dram", [NQ, TOK], BF16)
    xq_dram = nc.dram_tensor("xq_dram", [NQ, C], BF16)
    hq_dram = nc.dram_tensor("hq_dram", [NQ, I], BF16)
    x2_dram = nc.dram_tensor("x2_dram", [NQ, C], F32)
    agi_g = nc.dram_tensor("agi_g", [128, SHF], BF16)
    agi_u = nc.dram_tensor("agi_u", [128, SHF], BF16)
    agi_d = nc.dram_tensor("agi_d", [128, SHF], BF16)
    wgq = nc.dram_tensor("wgq", [I, C], BF16)
    wuq = nc.dram_tensor("wuq", [I, C], BF16)
    wdq = nc.dram_tensor("wdq", [C, I], BF16)
    ccin = nc.dram_tensor("ccin", [128, 48], F32)
    ccout = nc.dram_tensor("ccout", [128, 48], F32)

    ctx = contextlib.ExitStack()
    with ctx:
        sg = Sem(ctx.enter_context(nc.semaphore("sg")))
        st = Sem(ctx.enter_context(nc.semaphore("st")))
        ss = Sem(ctx.enter_context(nc.semaphore("ssd")))
        sm = Sem(ctx.enter_context(nc.semaphore("sm")))
        sv = Sem(ctx.enter_context(nc.semaphore("sv")))
        se = Sem(ctx.enter_context(nc.semaphore("se")))
        scc = Sem(ctx.enter_context(nc.semaphore("scc")))
        so = Sem(ctx.enter_context(nc.semaphore("so")))

        gamma = ctx.enter_context(nc.sbuf_tensor("gamma_sb", [128, 24], F32))
        beta = ctx.enter_context(nc.sbuf_tensor("beta_sb", [128, 24], F32))
        stA = ctx.enter_context(nc.sbuf_tensor("stA", [128, 24], F32))
        stB = ctx.enter_context(nc.sbuf_tensor("stB", [128, 24], F32))
        aqk = ctx.enter_context(nc.sbuf_tensor("aqk", [128, 8], F32))
        bqk = ctx.enter_context(nc.sbuf_tensor("bqk", [128, 8], F32))
        qpos = ctx.enter_context(nc.sbuf_tensor("qpos_sb", [128, 16], F32))
        onec = ctx.enter_context(nc.sbuf_tensor("onec", [128, 1], BF16))
        smx = ctx.enter_context(nc.sbuf_tensor("smx", [128, 64], F32))
        sqdump = ctx.enter_context(nc.sbuf_tensor("sqdump", [128, 1024], BF16))

        blk = ctx.enter_context(nc.Block())

        @blk.gpsimd
        def _(g):
            g.dma_start(out=wab.ap(), in_=w_attn.ap()).then_inc(sg.h, 16)
            g.dma_start(out=laa_b.ap()[:, 0:R], in_=laa.ap()).then_inc(sg.h, 16)
            g.dma_start(out=lab_b.ap(), in_=lab.ap()).then_inc(sg.h, 16)
        sg.n += 3 * 16
        WAB_C = sg.n

        @blk.gpsimd
        def _(g):
            g.dma_start(out=xp_b.ap(), in_=xin.ap()).then_inc(sg.h, 16)
        sg.n += 16
        XPB_C = sg.n

        @blk.gpsimd
        def _(g):
            g.dma_start(out=wpb.ap(), in_=w_proj.ap()).then_inc(sg.h, 16)
            g.dma_start(out=lpa_b.ap()[:, 0:R], in_=lpa.ap()).then_inc(sg.h, 16)
            g.dma_start(out=lpb_b.ap(), in_=lpb.ap()).then_inc(sg.h, 16)
            g.dma_start(out=onec[:, :], in_=ones_in.ap()).then_inc(sg.h, 16)
        sg.n += 4 * 16
        CASTS = sg.n

        @blk.scalar
        def _(s):
            s.dma_start(out=gamma[:, :], in_=gam2d.ap()).then_inc(ss.h, 16)
            s.dma_start(out=beta[:, :], in_=bet2d.ap()).then_inc(ss.h, 16)
            s.dma_start(out=qpos[:, :], in_=qpos_in.ap()).then_inc(ss.h, 16)
        ss.n += 3 * 16

        qstack = ctx.enter_context(contextlib.ExitStack())
        qraw = qstack.enter_context(nc.sbuf_tensor("qraw", [128, 8 * NQ], BF16))

        pA = qstack.enter_context(contextlib.ExitStack())
        xfm = pA.enter_context(nc.sbuf_tensor("xfm", [128, 8 * TOK], BF16))
        wmt = pA.enter_context(nc.sbuf_tensor("wmt", [128, 8 * D3], BF16))
        qsh = pA.enter_context(nc.sbuf_tensor("qsh", [128, SHQ], F32))
        qu = pA.enter_context(nc.sbuf_tensor("qu", [128, SHQ], F32))
        qg_ = pA.enter_context(nc.sbuf_tensor("qg_", [128, SHQ], BF16))
        qt_ = pA.enter_context(nc.sbuf_tensor("qt_", [128, SHQ], BF16))
        qs_ = pA.enter_context(nc.sbuf_tensor("qs_", [128, SHQ], BF16))
        qq_ = pA.enter_context(nc.sbuf_tensor("qq_", [128, SHQ], BF16))
        qbm = pA.enter_context(nc.sbuf_tensor("qbm", [128, SHQ // 16], F32))
        qsc = pA.enter_context(nc.sbuf_tensor("qsc", [128, SHQ // 16], F32))
        qrs = pA.enter_context(nc.sbuf_tensor("qrs", [128, SHQ // 16], F32))

        SHARD_SRCS = ((wg_sh, agi_g), (wu_sh, agi_u), (wd_sh, agi_d))

        def emit_shard_chunk(i):
            src_d, agi = SHARD_SRCS[i // 4]
            ch = i % 4

            @blk.scalar
            def _(s, src_d=src_d, ch=ch):
                s.wait_ge(sv.h, sv.n)
                s.dma_start(out=qsh[:, :],
                            in_=src_d.ap()[:, ch * SHQ:(ch + 1) * SHQ]).then_inc(ss.h, 16)
            ss.n += 16

            @blk.vector
            def _(v):
                v.wait_ge(ss.h, ss.n)

            @blk.scalar
            def _(s):
                s.wait_ge(ss.h, ss.n)

            emit_quant(blk, sv, se, qsh, 0, qu, qg_, qt_, qs_, qbm, qsc, qrs, qq_, 0, SHQ)

            @blk.scalar
            def _(s, agi=agi, ch=ch):
                s.wait_ge(sv.h, sv.n)
                s.dma_start(out=agi.ap()[:, ch * SHQ:(ch + 1) * SHQ],
                            in_=qq_[:, :]).then_inc(ss.h, 16)
            ss.n += 16

        aat = pA.enter_context(nc.sbuf_tensor("aat", [128, C], BF16))
        labs = pA.enter_context(nc.sbuf_tensor("labs", [64, D3], BF16))

        @blk.sync
        def _(sy):
            sy.wait_ge(sg.h, WAB_C)
            for c in range(8):
                sy.dma_start(wmt[:, c * D3:(c + 1) * D3],
                             wab.ap()[0:D3, c * 128:(c + 1) * 128],
                             transpose=True).then_inc(st.h, 16)
            sy.dma_start(aat[:, :], laa_b.ap()[0:C, 0:128], transpose=True).then_inc(st.h, 16)
            sy.dma_start(labs[:, :], lab_b.ap()).then_inc(st.h, 16)
            sy.wait_ge(sg.h, XPB_C)
            for c in range(8):
                sy.dma_start(xfm[:, c * TOK:(c + 1) * TOK],
                             xp_b.ap()[0:TOK, c * 128:(c + 1) * 128],
                             transpose=True).then_inc(st.h, 16)
        st.n += 18 * 16
        WLOADS = st.n

        with (nc.psum_tensor("psl0", [128, 512], F32) as psl0,
              nc.psum_tensor("psl1", [128, 512], F32) as psl1):
            psl = [psl0, psl1]
            lgate = Gate(sv, 2)
            for idx in range(48):
                jc, hf = idx // 2, idx % 2
                slot = idx % 2
                pb = psl[slot]

                @blk.tensor
                def _(t, pb=pb, idx=idx, jc=jc, hf=hf, slot=slot):
                    if idx == 0:
                        t.wait_ge(st.h, WLOADS - 8 * 16)
                    lgate.wait(t, slot)
                    ii = t.matmul(pb[:, :], labs[0:64, jc * 128:(jc + 1) * 128],
                                  aat[0:64, hf * 512:(hf + 1) * 512],
                                  start=True, stop=True)
                    sm.inc(ii)

                @blk.vector
                def _(v, pb=pb, jc=jc, hf=hf, slot=slot):
                    v.wait_ge(sm.h, sm.n)
                    s3, cch = jc // 8, jc % 8
                    o0 = cch * D3 + s3 + 3 * 512 * hf
                    tgt = wmt.ap()[:, o0:o0 + 3 * 511 + 1:3]
                    ii = v.tensor_tensor(tgt, tgt, pb[:, :], op=ALU.add)
                    sv.inc(ii)
                    lgate.release(slot)
        LORA_V = sv.n

        kvst = pA.enter_context(nc.sbuf_tensor("kvst", [128, 4 * 512], BF16))
        vsq = pA.enter_context(nc.sbuf_tensor("vsq", [128, 4 * 512], BF16))
        qp_s = pA.enter_context(nc.sbuf_tensor("qp_s", [128, 32], F32))
        qp_q = pA.enter_context(nc.sbuf_tensor("qp_q", [128, 32], F32))
        kp_s = pA.enter_context(nc.sbuf_tensor("kp_s", [128, 32], F32))
        kp_q = pA.enter_context(nc.sbuf_tensor("kp_q", [128, 32], F32))
        stat48 = pA.enter_context(nc.sbuf_tensor("stat48", [128, 48], F32))

        with (nc.psum_tensor("pr0", [128, 512], F32) as pr0,
              nc.psum_tensor("pr1", [128, 512], F32) as pr1,
              nc.psum_tensor("pr2", [128, 512], F32) as pr2,
              nc.psum_tensor("pr3", [128, 512], F32) as pr3,
              nc.psum_tensor("psv", [128, 16], F32) as psv):
            prr = [pr0, pr1, pr2, pr3]
            bgate = Gate(se, 4)
            vgate = Gate(sv, 4)
            kstg = Gate(st, 4)

            for it in range(64):
                tile, dg = it // 2, it % 2
                bank = it % 4
                stg = it % 4
                pb = prr[bank]

                @blk.tensor
                def _(t, pb=pb, tile=tile, dg=dg, bank=bank, it=it):
                    if it == 0:
                        t.wait_ge(sv.h, LORA_V)
                    vgate.wait(t, bank)
                    for cch in range(8):
                        ii = t.matmul(pb[:, :],
                                      xfm[:, cch * TOK + tile * 128:cch * TOK + (tile + 1) * 128],
                                      wmt[:, cch * D3 + 2048 + dg * 512:cch * D3 + 2048 + (dg + 1) * 512],
                                      start=(cch == 0), stop=(cch == 7))
                    sm.inc(ii)

                @blk.vector
                def _(v, pb=pb, stg=stg, bank=bank):
                    v.wait_ge(sm.h, sm.n)
                    kstg.wait(v, stg)
                    ii = v.tensor_copy(kvst[:, stg * 512:(stg + 1) * 512], pb[:, :])
                    sv.inc(ii)
                    vgate.release(bank)

                @blk.scalar
                def _(s, stg=stg, tile=tile, dg=dg):
                    s.wait_ge(sv.h, sv.n)
                    s.dma_start(out=v_dram.ap()[tile * 128:(tile + 1) * 128, dg * 512:(dg + 1) * 512],
                                in_=kvst[:, stg * 512:(stg + 1) * 512]).then_inc(st.h, 16)
                st.n += 16
                kstg.release(stg)
                if it % 4 == 0 and it // 4 < 12:
                    emit_shard_chunk(it // 4)

            AGI_STORED = ss.n

            @blk.gpsimd
            def _(g):
                g.wait_ge(ss.h, AGI_STORED)
                for i_, o_ in ((agi_g, wgq), (agi_u, wuq), (agi_d, wdq)):
                    ii = g.collective_compute("AllGather", ALU.bypass,
                                              replica_groups=[list(range(NCORE))],
                                              ins=[i_.ap()], outs=[o_.ap()])
                    scc.inc(ii)
            AG_DONE = scc.n

            for it in range(64):
                tg, d = it // 8, it % 8
                bank = it % 4
                stg = it % 4
                own = tg < 4
                pb = prr[bank]

                @blk.tensor
                def _(t, pb=pb, tg=tg, d=d, bank=bank):
                    bgate.wait(t, bank)
                    vgate.wait(t, bank)
                    for cch in range(8):
                        ii = t.matmul(pb[:, :],
                                      wmt[:, cch * D3 + 1024 + d * 128:cch * D3 + 1024 + (d + 1) * 128],
                                      xfm[:, cch * TOK + tg * 512:cch * TOK + (tg + 1) * 512],
                                      start=(cch == 0), stop=(cch == 7))
                    sm.inc(ii)

                if own:
                    @blk.scalar
                    def _(s, pb=pb, tg=tg, d=d, bank=bank, stg=stg):
                        s.wait_ge(sm.h, sm.n)
                        kstg.wait(s, stg)
                        s.activation(kvst[:, stg * 512:(stg + 1) * 512], pb[:, :], AF.Copy,
                                     accum_out=kp_s[:, d * 4 + tg:d * 4 + tg + 1])
                        ii = s.activation(sqdump[:, 0:512], pb[:, :], AF.Square,
                                          accum_out=kp_q[:, d * 4 + tg:d * 4 + tg + 1])
                        se.inc(ii)
                        bgate.release(bank)
                        s.wait_ge(se.h, se.n)
                        s.dma_start(out=k_dram.ap()[d * 128:(d + 1) * 128, tg * 512:(tg + 1) * 512],
                                    in_=kvst[:, stg * 512:(stg + 1) * 512]).then_inc(st.h, 16)
                else:
                    @blk.vector
                    def _(v, pb=pb, stg=stg, bank=bank):
                        v.wait_ge(sm.h, sm.n)
                        kstg.wait(v, stg)
                        ii = v.tensor_copy(kvst[:, stg * 512:(stg + 1) * 512], pb[:, :])
                        sv.inc(ii)
                        vgate.release(bank)

                    @blk.scalar
                    def _(s, tg=tg, d=d, stg=stg):
                        s.wait_ge(sv.h, sv.n)
                        s.dma_start(out=k_dram.ap()[d * 128:(d + 1) * 128, tg * 512:(tg + 1) * 512],
                                    in_=kvst[:, stg * 512:(stg + 1) * 512]).then_inc(st.h, 16)
                st.n += 16
                kstg.release(stg)

            VSTATS_ST = st.n
            sq_se = [0] * 32
            mm_sm = [0] * 32

            def emit_statsmm(step):
                ti, dg = step // 2, step % 2
                sl4 = step % 4

                @blk.tensor
                def _(t, ti=ti, dg=dg, sl4=sl4, step=step):
                    t.wait_ge(se.h, sq_se[step])
                    for sl in range(4):
                        t.matmul(psv[:, dg * 4 + sl:dg * 4 + sl + 1],
                                 kvst[:, sl4 * 512 + sl * 128:sl4 * 512 + (sl + 1) * 128],
                                 onec[:, 0:1],
                                 start=(ti == 0), stop=(ti == 15))
                        ii = t.matmul(psv[:, 8 + dg * 4 + sl:9 + dg * 4 + sl],
                                      vsq[:, sl4 * 512 + sl * 128:sl4 * 512 + (sl + 1) * 128],
                                      onec[:, 0:1],
                                      start=(ti == 0), stop=(ti == 15))
                    sm.inc(ii)
                mm_sm[step] = sm.n

            for it in range(32):
                tg, d = it // 8, it % 8
                bank = it % 4
                pb = prr[bank]

                @blk.tensor
                def _(t, pb=pb, tg=tg, d=d, bank=bank):
                    bgate.wait(t, bank)
                    vgate.wait(t, bank)
                    for cch in range(8):
                        ii = t.matmul(pb[:, :],
                                      wmt[:, cch * D3 + d * 128:cch * D3 + (d + 1) * 128],
                                      xfm[:, cch * TOK + tg * 512:cch * TOK + (tg + 1) * 512],
                                      start=(cch == 0), stop=(cch == 7))
                    sm.inc(ii)

                if it >= 4:
                    emit_statsmm(it - 4)

                step, sl4 = it, it % 4
                ti2, dg2 = step // 2, step % 2

                @blk.sync
                def _(sy, ti2=ti2, dg2=dg2, sl4=sl4, step=step):
                    if step < 4:
                        sy.wait_ge(st.h, VSTATS_ST)
                    else:
                        sy.wait_ge(sm.h, mm_sm[step - 4])
                    sy.dma_start(out=kvst[:, sl4 * 512:(sl4 + 1) * 512],
                                 in_=v_dram.ap()[ti2 * 128:(ti2 + 1) * 128, dg2 * 512:(dg2 + 1) * 512]
                                 ).then_inc(st.h, 16)
                st.n += 16
                ld_st = st.n

                @blk.scalar
                def _(s, pb=pb, tg=tg, d=d, bank=bank, sl4=sl4, step=step, ld_st=ld_st):
                    s.wait_ge(sm.h, sm.n)
                    s.activation(qraw[:, d * NQ + tg * 512:d * NQ + (tg + 1) * 512],
                                 pb[:, :], AF.Copy,
                                 accum_out=qp_s[:, d * 4 + tg:d * 4 + tg + 1])
                    ii = s.activation(sqdump[:, 0:512], pb[:, :], AF.Square,
                                      accum_out=qp_q[:, d * 4 + tg:d * 4 + tg + 1])
                    se.inc(ii)
                    bgate.release(bank)
                    s.wait_ge(st.h, ld_st)
                    ii = s.square(vsq[:, sl4 * 512:(sl4 + 1) * 512],
                                  kvst[:, sl4 * 512:(sl4 + 1) * 512])
                    se.inc(ii)
                    sq_se[step] = se.n

            for step in range(28, 32):
                emit_statsmm(step)

            @blk.vector
            def _(v):
                v.wait_ge(sm.h, sm.n)
                v.wait_ge(se.h, se.n)
                for d in range(8):
                    v.tensor_reduce(smx[:, d:d + 1], qp_s[:, d * 4:(d + 1) * 4], axis=AXX, op=ALU.add)
                    v.tensor_reduce(smx[:, 24 + d:25 + d], qp_q[:, d * 4:(d + 1) * 4], axis=AXX, op=ALU.add)
                    v.tensor_reduce(smx[:, 8 + d:9 + d], kp_s[:, d * 4:(d + 1) * 4], axis=AXX, op=ALU.add)
                    v.tensor_reduce(smx[:, 32 + d:33 + d], kp_q[:, d * 4:(d + 1) * 4], axis=AXX, op=ALU.add)
                v.tensor_copy(smx[:, 16:24], psv[:, 0:8])
                ii = v.tensor_copy(smx[:, 40:48], psv[:, 8:16])
                sv.inc(ii)

        @blk.scalar
        def _(s):
            s.wait_ge(sv.h, sv.n)
            s.dma_start(out=ccin.ap(), in_=smx[:, 0:48]).then_inc(ss.h, 16)
        ss.n += 16

        @blk.gpsimd
        def _(g):
            g.wait_ge(ss.h, ss.n)
            ii = g.collective_compute("AllReduce", ALU.add,
                                      replica_groups=[list(range(NCORE))],
                                      ins=[ccin.ap()], outs=[ccout.ap()])
            scc.inc(ii)

        @blk.scalar
        def _(s):
            s.wait_ge(scc.h, scc.n)
            s.dma_start(out=stat48[:, :], in_=ccout.ap()).then_inc(ss.h, 16)
        ss.n += 16

        @blk.vector
        def _(v):
            v.wait_ge(ss.h, ss.n)
            ii = v.tensor_scalar(smx[:, 0:24], stat48[:, 0:24], INV16384, None, op0=ALU.mult)
            sv.inc(ii)
            v.wait_ge(sv.h, sv.n)
            ii = v.tensor_scalar(smx[:, 24:48], stat48[:, 24:48], INV16384, None, op0=ALU.mult)
            sv.inc(ii)
            v.wait_ge(sv.h, sv.n)
            ii = v.tensor_tensor(stat48[:, 0:24], smx[:, 0:24], smx[:, 0:24], op=ALU.mult)
            sv.inc(ii)
            v.wait_ge(sv.h, sv.n)
            ii = v.tensor_tensor(stat48[:, 24:48], smx[:, 24:48], stat48[:, 0:24], op=ALU.subtract)
            sv.inc(ii)
            v.wait_ge(sv.h, sv.n)
            ii = v.tensor_scalar(stat48[:, 24:48], stat48[:, 24:48], EPS, None, op0=ALU.add)
            sv.inc(ii)

        @blk.scalar
        def _(s):
            s.wait_ge(sv.h, sv.n)
            ii = s.activation(stat48[:, 0:24], stat48[:, 24:48], AF.Sqrt)
            se.inc(ii)

        @blk.vector
        def _(v):
            v.wait_ge(se.h, se.n)
            ii = v.reciprocal(stat48[:, 24:48], stat48[:, 0:24])
            sv.inc(ii)
            v.wait_ge(sv.h, sv.n)
            ii = v.tensor_tensor(stA[:, :], gamma[:, :], stat48[:, 24:48], op=ALU.mult)
            sv.inc(ii)
            v.wait_ge(sv.h, sv.n)
            ii = v.tensor_tensor(smx[:, 0:24], smx[:, 0:24], stA[:, :], op=ALU.mult)
            sv.inc(ii)
            v.wait_ge(sv.h, sv.n)
            ii = v.tensor_tensor(stB[:, :], beta[:, :], smx[:, 0:24], op=ALU.subtract)
            sv.inc(ii)
            v.wait_ge(sv.h, sv.n)
            ii = v.tensor_tensor(aqk[:, :], stA[:, 0:8], stA[:, 8:16], op=ALU.mult)
            sv.inc(ii)
            v.wait_ge(sv.h, sv.n)
            ii = v.tensor_tensor(bqk[:, :], stB[:, 0:8], stA[:, 8:16], op=ALU.mult)
            sv.inc(ii)
            v.wait_ge(sv.h, sv.n)
            for cch in range(8):
                ii = v.tensor_scalar(qraw[:, cch * NQ:(cch + 1) * NQ],
                                     qraw[:, cch * NQ:(cch + 1) * NQ],
                                     aqk[:, cch:cch + 1], bqk[:, cch:cch + 1],
                                     op0=ALU.mult, op1=ALU.add)
                sv.inc(ii)
        QA_V = sv.n
        KV_STORED = st.n
        P1_SM = sm.n
        P1_SE = se.n

        pA.close()

        pB = qstack.enter_context(contextlib.ExitStack())
        kidxp = pB.enter_context(nc.sbuf_tensor("kidxp", [128, TOK], F32))
        kbuf = pB.enter_context(nc.sbuf_tensor("kbuf", [128, 2 * 4096], BF16))
        masks = pB.enter_context(nc.sbuf_tensor("masks", [128, 4 * TOK], BF16))
        strips = pB.enter_context(nc.sbuf_tensor("strips", [128, 4 * TOK], F32))
        pstr = pB.enter_context(nc.sbuf_tensor("pstr", [128, 2 * TOK], BF16))
        pnrm = pB.enter_context(nc.sbuf_tensor("pnrm", [128, 2 * TOK], BF16))

        @blk.sync
        def _(sy):
            sy.wait_ge(st.h, KV_STORED)
            sy.wait_ge(sm.h, P1_SM)
            sy.wait_ge(se.h, P1_SE)
            sy.wait_ge(sv.h, QA_V)
            sy.dma_start(out=kidxp[:, :], in_=kidx_in.ap()).then_inc(st.h, 16)
        st.n += 16
        KIDX_L = st.n

        with (nc.psum_tensor("sp0", [128, 512], F32) as sp0,
              nc.psum_tensor("sp1", [128, 512], F32) as sp1,
              nc.psum_tensor("sp2", [128, 512], F32) as sp2,
              nc.psum_tensor("sp3", [128, 512], F32) as sp3,
              nc.psum_tensor("sp4", [128, 512], F32) as sp4,
              nc.psum_tensor("sp5", [128, 512], F32) as sp5,
              nc.psum_tensor("sp6", [128, 512], F32) as sp6,
              nc.psum_tensor("sp7", [128, 512], F32) as sp7):
            spb = [sp0, sp1, sp2, sp3, sp4, sp5, sp6, sp7]
            sgate = Gate(sv, 8)
            kgate = Gate(sm, 2)

            def emit_masks(qg):
                @blk.vector
                def _(v, qg=qg):
                    if qg == 0:
                        v.wait_ge(st.h, KIDX_L)
                        v.wait_ge(sv.h, QA_V)
                    for tt in range(4):
                        tglob = qg * 4 + tt
                        ii = v.tensor_scalar(masks[:, tt * TOK:(tt + 1) * TOK], kidxp[:, :],
                                             qpos[:, tglob:tglob + 1], NEGBIG,
                                             op0=ALU.is_gt, op1=ALU.mult)
                        sv.inc(ii)

            emit_masks(0)
            nrm_sv = [0] * 4
            for qg in range(4):
                for ks in range(8):
                    kb = ks % 2

                    @blk.sync
                    def _(sy, qg=qg, ks=ks, kb=kb):
                        if qg == 0 and ks == 0:
                            sy.wait_ge(st.h, KV_STORED)
                        kgate.wait(sy, kb)
                        sy.dma_start(
                            kbuf.ap()[:, kb * 4096:(kb + 1) * 4096].rearrange("p (n k) -> p n k", k=512),
                            k_dram.ap()[:, ks * 512:(ks + 1) * 512].rearrange("(n p) k -> p n k", p=128),
                        ).then_inc(st.h, 16)
                    st.n += 16

                    for tt in range(4):
                        bank = kb * 4 + tt
                        pb = spb[bank]

                        @blk.tensor
                        def _(t, pb=pb, qg=qg, ks=ks, tt=tt, kb=kb, bank=bank):
                            if qg == 0 and ks == 0 and tt == 0:
                                t.wait_ge(sv.h, QA_V)
                            t.wait_ge(st.h, st.n)
                            sgate.wait(t, bank)
                            tglob = qg * 4 + tt
                            for cch in range(8):
                                ii = t.matmul(pb[:, :],
                                              qraw[:, cch * NQ + tglob * 128:cch * NQ + (tglob + 1) * 128],
                                              kbuf[:, kb * 4096 + cch * 512:kb * 4096 + (cch + 1) * 512],
                                              start=(cch == 0), stop=(cch == 7))
                            sm.inc(ii)
                            if tt == 3:
                                kgate.release(kb)

                        @blk.vector
                        def _(v, pb=pb, ks=ks, tt=tt, bank=bank):
                            v.wait_ge(sm.h, sm.n)
                            ii = v.scalar_tensor_tensor(
                                strips[:, tt * TOK + ks * 512:tt * TOK + (ks + 1) * 512],
                                pb[:, :], SMSCALE,
                                masks[:, tt * TOK + ks * 512:tt * TOK + (ks + 1) * 512],
                                op0=ALU.mult, op1=ALU.add)
                            sv.inc(ii)
                            sgate.release(bank)

                red_sv = [0] * 4
                for tt in range(4):
                    @blk.vector
                    def _(v, tt=tt):
                        v.wait_ge(sv.h, sv.n)
                        ii = v.tensor_reduce(smx[:, 48 + tt:49 + tt], strips[:, tt * TOK:(tt + 1) * TOK],
                                             axis=AXX, op=ALU.max, negate=True)
                        sv.inc(ii)
                        red_sv[tt] = sv.n

                for tt in range(4):
                    @blk.scalar
                    def _(s, tt=tt):
                        s.wait_ge(sv.h, red_sv[tt])
                        s.wait_ge(sv.h, nrm_sv[(tt + 2) % 4])
                        ii = s.activation(pstr[:, (tt % 2) * TOK:(tt % 2 + 1) * TOK],
                                          strips[:, tt * TOK:(tt + 1) * TOK], AF.Exp,
                                          bias=smx[:, 48 + tt:49 + tt], scale=1.0,
                                          accum_out=smx[:, 52 + tt:53 + tt])
                        se.inc(ii)

                    @blk.vector
                    def _(v, tt=tt):
                        v.wait_ge(se.h, se.n)
                        v.wait_ge(st.h, st.n)
                        ii = v.reciprocal(smx[:, 56 + tt:57 + tt], smx[:, 52 + tt:53 + tt])
                        sv.inc(ii)
                        v.wait_ge(sv.h, sv.n)
                        ii = v.tensor_scalar(pnrm[:, (tt % 2) * TOK:(tt % 2 + 1) * TOK],
                                             pstr[:, (tt % 2) * TOK:(tt % 2 + 1) * TOK],
                                             smx[:, 56 + tt:57 + tt], None, op0=ALU.mult)
                        sv.inc(ii)
                        nrm_sv[tt] = sv.n

                    @blk.sync
                    def _(sy, qg=qg, tt=tt):
                        sy.wait_ge(sv.h, sv.n)
                        tglob = qg * 4 + tt
                        sy.dma_start(out=p_dram.ap()[tglob * 128:(tglob + 1) * 128, :],
                                     in_=pnrm[:, (tt % 2) * TOK:(tt % 2 + 1) * TOK]).then_inc(st.h, 16)
                    st.n += 16

                    if qg < 3:
                        @blk.vector
                        def _(v, qg=qg, tt=tt):
                            tglob = (qg + 1) * 4 + tt
                            ii = v.tensor_scalar(masks[:, tt * TOK:(tt + 1) * TOK],
                                                 kidxp[:, :],
                                                 qpos[:, tglob:tglob + 1], NEGBIG,
                                                 op0=ALU.is_gt, op1=ALU.mult)
                            sv.inc(ii)
        P_STORED = st.n
        P2_SM_END = sm.n
        pB.close()
        qstack.close()

        p3 = ctx.enter_context(contextlib.ExitStack())
        wpt = p3.enter_context(nc.sbuf_tensor("wpt", [128, 8 * C], BF16))
        wmpt = p3.enter_context(nc.sbuf_tensor("wmpt", [128, 8 * C], BF16))
        apt = p3.enter_context(nc.sbuf_tensor("apt", [128, C], BF16))
        lpbs = p3.enter_context(nc.sbuf_tensor("lpbs", [64, C], BF16))
        ptbuf = p3.enter_context(nc.sbuf_tensor("ptbuf", [128, 32 * 1024], BF16))
        vbuf = p3.enter_context(nc.sbuf_tensor("vbuf", [128, 3 * 512], BF16))
        aout = p3.enter_context(nc.sbuf_tensor("aout", [128, 8 * 1024], BF16))
        y1 = p3.enter_context(nc.sbuf_tensor("y1", [128, 8 * 1024], BF16))
        vres = p3.enter_context(nc.sbuf_tensor("vres", [128, 2 * 1024], F32))
        xtile = p3.enter_context(nc.sbuf_tensor("xtile", [128, 2 * 1024], F32))
        x2t = p3.enter_context(nc.sbuf_tensor("x2t", [128, 2 * 1024], F32))
        q_u = p3.enter_context(nc.sbuf_tensor("q_u", [128, 1024], F32))
        q_qg = p3.enter_context(nc.sbuf_tensor("q_qg", [128, 1024], BF16))
        q_t0 = p3.enter_context(nc.sbuf_tensor("q_t0", [128, 1024], BF16))
        q_sg = p3.enter_context(nc.sbuf_tensor("q_sg", [128, 1024], BF16))
        q_bm = p3.enter_context(nc.sbuf_tensor("q_bm", [128, 64], F32))
        q_sc = p3.enter_context(nc.sbuf_tensor("q_sc", [128, 64], F32))
        q_rs = p3.enter_context(nc.sbuf_tensor("q_rs", [128, 64], F32))
        q_out = p3.enter_context(nc.sbuf_tensor("q_out", [128, 2 * 1024], BF16))
        lnt = p3.enter_context(nc.sbuf_tensor("lnt", [128, 16], F32))

        @blk.sync
        def _(sy):
            sy.wait_ge(sm.h, P2_SM_END)
            sy.wait_ge(st.h, P_STORED)
            for c in range(8):
                sy.dma_start(wpt[:, c * C:(c + 1) * C],
                             wpb.ap()[0:C, c * 128:(c + 1) * 128],
                             transpose=True).then_inc(st.h, 16)
            sy.dma_start(apt[:, :], lpa_b.ap()[0:C, 0:128], transpose=True).then_inc(st.h, 16)
            sy.dma_start(lpbs[:, :], lpb_b.ap()).then_inc(st.h, 16)
        st.n += 10 * 16
        WP_L = st.n

        with (nc.psum_tensor("psp0", [128, 512], F32) as psp0,
              nc.psum_tensor("psp1", [128, 512], F32) as psp1):
            psp = [psp0, psp1]
            pgate = Gate(sv, 2)
            for idx in range(16):
                jc, hf = idx // 2, idx % 2
                slot = idx % 2
                pb = psp[slot]

                @blk.tensor
                def _(t, pb=pb, idx=idx, jc=jc, hf=hf, slot=slot):
                    if idx == 0:
                        t.wait_ge(st.h, WP_L)
                    pgate.wait(t, slot)
                    ii = t.matmul(pb[:, :], lpbs[0:64, jc * 128:(jc + 1) * 128],
                                  apt[0:64, hf * 512:(hf + 1) * 512],
                                  start=True, stop=True)
                    sm.inc(ii)

                @blk.vector
                def _(v, pb=pb, jc=jc, hf=hf, slot=slot):
                    v.wait_ge(sm.h, sm.n)
                    ii = v.tensor_tensor(
                        wmpt[:, jc * C + hf * 512:jc * C + (hf + 1) * 512],
                        wpt[:, jc * C + hf * 512:jc * C + (hf + 1) * 512],
                        pb[:, :], op=ALU.add)
                    sv.inc(ii)
                    pgate.release(slot)
        WMPT_V = sv.n

        with (nc.psum_tensor("pv0", [128, 512], F32) as pv0,
              nc.psum_tensor("pv1", [128, 512], F32) as pv1,
              nc.psum_tensor("pv2", [128, 512], F32) as pv2,
              nc.psum_tensor("pv3", [128, 512], F32) as pv3,
              nc.psum_tensor("pv4", [128, 512], F32) as pv4,
              nc.psum_tensor("pv5", [128, 512], F32) as pv5,
              nc.psum_tensor("pv6", [128, 512], F32) as pv6,
              nc.psum_tensor("pv7", [128, 512], F32) as pv7):
            pvb = [pv0, pv1, pv2, pv3, pv4, pv5, pv6, pv7]
            vslotg = Gate(sm, 3)
            pvg = Gate(sv, 8)
            p1g = Gate(se, 8)
            p2g = Gate(sv, 8)
            xtg = Gate(sv, 2)
            x2g = Gate(ss, 2)

            PV_SM_END = [0]
            for half in range(2):
                HALF_SV = sv.n
                HALF_SE = se.n

                @blk.sync
                def _(sy, half=half):
                    if half == 0:
                        sy.wait_ge(st.h, P_STORED)
                    else:
                        sy.wait_ge(sm.h, PV_SM_END[0])
                    for kc in range(32):
                        sy.dma_start(ptbuf[:, kc * 1024:(kc + 1) * 1024],
                                     p_dram.ap()[half * 1024:(half + 1) * 1024, kc * 128:(kc + 1) * 128],
                                     transpose=True).then_inc(st.h, 16)
                st.n += 32 * 16

                for cg in range(2):
                    for kc in range(32):
                        vslot = kc % 3

                        @blk.sync
                        def _(sy, half=half, cg=cg, kc=kc, vslot=vslot):
                            if half == 0 and cg == 0 and kc == 0:
                                sy.wait_ge(st.h, KV_STORED)
                            vslotg.wait(sy, vslot)
                            sy.dma_start(vbuf[:, vslot * 512:(vslot + 1) * 512],
                                         v_dram.ap()[kc * 128:(kc + 1) * 128, cg * 512:(cg + 1) * 512]
                                         ).then_inc(st.h, 16)
                        st.n += 16

                        @blk.tensor
                        def _(t, cg=cg, kc=kc, vslot=vslot, HALF_SV=HALF_SV, HALF_SE=HALF_SE):
                            t.wait_ge(st.h, st.n)
                            if kc == 0:
                                t.wait_ge(sv.h, HALF_SV)
                                t.wait_ge(se.h, HALF_SE)
                                for bb in range(8):
                                    pvg.wait(t, bb)
                            for cc in range(4):
                                for qq in range(2):
                                    ii = t.matmul(pvb[cc * 2 + qq][:, :],
                                                  vbuf[:, vslot * 512 + cc * 128:vslot * 512 + (cc + 1) * 128],
                                                  ptbuf[:, kc * 1024 + qq * 512:kc * 1024 + (qq + 1) * 512],
                                                  start=(kc == 0), stop=(kc == 31))
                            sm.inc(ii)
                            vslotg.release(vslot)

                    @blk.vector
                    def _(v, cg=cg):
                        v.wait_ge(sm.h, sm.n)
                        for cc in range(4):
                            cglob = cg * 4 + cc
                            for qq in range(2):
                                ii = v.tensor_scalar(
                                    aout[:, cglob * 1024 + qq * 512:cglob * 1024 + (qq + 1) * 512],
                                    pvb[cc * 2 + qq][:, :],
                                    stA[:, 16 + cglob:17 + cglob], stB[:, 16 + cglob:17 + cglob],
                                    op0=ALU.mult, op1=ALU.add)
                        sv.inc(ii)
                        for bb in range(8):
                            pvg.release(bb)
                AOUT_V = sv.n
                PV_SM_END[0] = sm.n

                for cp in range(8):
                    b0, b1 = (cp * 2) % 8, (cp * 2 + 1) % 8

                    @blk.tensor
                    def _(t, cp=cp, b0=b0, b1=b1, AOUT_V=AOUT_V):
                        if cp == 0:
                            t.wait_ge(sv.h, AOUT_V)
                            t.wait_ge(st.h, WP_L)
                        p1g.wait(t, b0)
                        p1g.wait(t, b1)
                        for qq in range(2):
                            pb = pvb[b0 if qq == 0 else b1]
                            for cc in range(8):
                                ii = t.matmul(pb[:, :],
                                              wpt[:, cc * C + cp * 128:cc * C + (cp + 1) * 128],
                                              aout[:, cc * 1024 + qq * 512:cc * 1024 + (qq + 1) * 512],
                                              start=(cc == 0), stop=(cc == 7))
                        sm.inc(ii)

                    @blk.scalar
                    def _(s, cp=cp, b0=b0, b1=b1):
                        s.wait_ge(sm.h, sm.n)
                        s.activation(y1[:, cp * 1024:cp * 1024 + 512], pvb[b0][:, :], AF.Copy)
                        ii = s.activation(y1[:, cp * 1024 + 512:(cp + 1) * 1024], pvb[b1][:, :], AF.Copy)
                        se.inc(ii)
                        p1g.release(b0)
                        p1g.release(b1)
                Y1_S = se.n

                for tt in range(8):
                    gt = half * 8 + tt
                    sl = tt % 2
                    b0, b1 = (tt * 2) % 8, (tt * 2 + 1) % 8

                    @blk.scalar
                    def _(s, gt=gt, sl=sl, half=half, tt=tt):
                        if half == 0 and tt == 0:
                            s.wait_ge(st.h, P_STORED)
                        xtg.wait(s, sl)
                        s.dma_start(out=xtile[:, sl * 1024:(sl + 1) * 1024],
                                    in_=xin.ap()[gt * 128:(gt + 1) * 128, :]).then_inc(ss.h, 16)
                    ss.n += 16
                    XT_L = ss.n

                    @blk.tensor
                    def _(t, tt=tt, b0=b0, b1=b1, Y1_S=Y1_S):
                        if tt == 0:
                            t.wait_ge(se.h, Y1_S)
                        p2g.wait(t, b0)
                        p2g.wait(t, b1)
                        for dg in range(2):
                            pb = pvb[b0 if dg == 0 else b1]
                            for cc in range(8):
                                ii = t.matmul(pb[:, :],
                                              y1[:, cc * 1024 + tt * 128:cc * 1024 + (tt + 1) * 128],
                                              wmpt[:, cc * C + dg * 512:cc * C + (dg + 1) * 512],
                                              start=(cc == 0), stop=(cc == 7))
                        sm.inc(ii)

                    @blk.vector
                    def _(v, sl=sl, b0=b0, b1=b1, XT_L=XT_L):
                        v.wait_ge(sm.h, sm.n)
                        v.wait_ge(ss.h, XT_L)
                        x2g.wait(v, sl)
                        v.tensor_tensor(vres[:, sl * 1024:sl * 1024 + 512], pvb[b0][:, :],
                                        xtile[:, sl * 1024:sl * 1024 + 512], op=ALU.add)
                        ii = v.tensor_tensor(vres[:, sl * 1024 + 512:(sl + 1) * 1024], pvb[b1][:, :],
                                             xtile[:, sl * 1024 + 512:(sl + 1) * 1024], op=ALU.add)
                        sv.inc(ii)
                        p2g.release(b0)
                        p2g.release(b1)
                        xtg.release(sl)

                    emit_ln(blk, sv, se, vres, x2t, lnt, sl, sqdump)

                    @blk.scalar
                    def _(s, gt=gt, sl=sl):
                        s.wait_ge(sv.h, sv.n)
                        s.dma_start(out=x2_dram.ap()[gt * 128:(gt + 1) * 128, :],
                                    in_=x2t[:, sl * 1024:(sl + 1) * 1024]).then_inc(ss.h, 16)
                    ss.n += 16

                    emit_quant(blk, sv, se, x2t, sl * 1024, q_u, q_qg, q_t0, q_sg,
                               q_bm, q_sc, q_rs, q_out, sl * 1024, 1024)

                    @blk.scalar
                    def _(s, gt=gt, sl=sl):
                        s.wait_ge(sv.h, sv.n)
                        s.dma_start(out=xq_dram.ap()[gt * 128:(gt + 1) * 128, :],
                                    in_=q_out[:, sl * 1024:(sl + 1) * 1024]).then_inc(ss.h, 16)
                    ss.n += 16
                    x2g.release(sl)
        XQ_STORED = ss.n
        p3.close()

        p4 = ctx.enter_context(contextlib.ExitStack())
        xqfm = p4.enter_context(nc.sbuf_tensor("xqfm", [128, 8 * NQ], BF16))
        wgt = p4.enter_context(nc.sbuf_tensor("wgt", [128, 8 * I], BF16))
        wut = p4.enter_context(nc.sbuf_tensor("wut", [128, 8 * I], BF16))
        silu = p4.enter_context(nc.sbuf_tensor("silu", [128, 512], F32))
        htile = p4.enter_context(nc.sbuf_tensor("htile", [128, 2 * I], F32))
        h_u = p4.enter_context(nc.sbuf_tensor("h_u", [128, I], F32))
        h_qg = p4.enter_context(nc.sbuf_tensor("h_qg", [128, I], BF16))
        h_t0 = p4.enter_context(nc.sbuf_tensor("h_t0", [128, I], BF16))
        h_sg = p4.enter_context(nc.sbuf_tensor("h_sg", [128, I], BF16))
        h_bm = p4.enter_context(nc.sbuf_tensor("h_bm", [128, I // 16], F32))
        h_sc = p4.enter_context(nc.sbuf_tensor("h_sc", [128, I // 16], F32))
        h_rs = p4.enter_context(nc.sbuf_tensor("h_rs", [128, I // 16], F32))
        h_out = p4.enter_context(nc.sbuf_tensor("h_out", [128, 2 * I], BF16))

        @blk.sync
        def _(sy):
            sy.wait_ge(ss.h, XQ_STORED)
            sy.wait_ge(scc.h, AG_DONE)
            for c in range(8):
                sy.dma_start(xqfm[:, c * NQ:(c + 1) * NQ],
                             xq_dram.ap()[0:NQ, c * 128:(c + 1) * 128],
                             transpose=True).then_inc(st.h, 16)
            for c in range(8):
                sy.dma_start(wgt[:, c * I:(c + 1) * I],
                             wgq.ap()[0:I, c * 128:(c + 1) * 128],
                             transpose=True).then_inc(st.h, 16)
            for c in range(8):
                sy.dma_start(wut[:, c * I:(c + 1) * I],
                             wuq.ap()[0:I, c * 128:(c + 1) * 128],
                             transpose=True).then_inc(st.h, 16)
        st.n += 24 * 16
        FFNW_L = st.n

        IGRP = [(0, 512), (512, 512), (1024, 512), (1536, 512), (2048, 512), (2560, 256)]
        with (nc.psum_tensor("fg0", [128, 512], F32) as fg0,
              nc.psum_tensor("fg1", [128, 512], F32) as fg1,
              nc.psum_tensor("fu0", [128, 512], F32) as fu0,
              nc.psum_tensor("fu1", [128, 512], F32) as fu1):
            fgb, fub = [fg0, fg1], [fu0, fu1]
            fgate = Gate(sv, 2)
            hqg = Gate(ss, 2)

            for tile in range(16):
                sl = tile % 2
                for gi, (ib, iw) in enumerate(IGRP):
                    pb = gi % 2

                    @blk.tensor
                    def _(t, tile=tile, gi=gi, ib=ib, iw=iw, pb=pb):
                        if tile == 0 and gi == 0:
                            t.wait_ge(st.h, FFNW_L - 8 * 16)
                        fgate.wait(t, pb)
                        for cc in range(8):
                            ii = t.matmul(fgb[pb][:, 0:iw],
                                          xqfm[:, cc * NQ + tile * 128:cc * NQ + (tile + 1) * 128],
                                          wgt[:, cc * I + ib:cc * I + ib + iw],
                                          start=(cc == 0), stop=(cc == 7))
                        sm.inc(ii)
                        if tile == 0 and gi == 0:
                            t.wait_ge(st.h, FFNW_L)
                        for cc in range(8):
                            ii = t.matmul(fub[pb][:, 0:iw],
                                          xqfm[:, cc * NQ + tile * 128:cc * NQ + (tile + 1) * 128],
                                          wut[:, cc * I + ib:cc * I + ib + iw],
                                          start=(cc == 0), stop=(cc == 7))
                        sm.inc(ii)

                    @blk.scalar
                    def _(s, gi=gi, pb=pb, iw=iw):
                        s.wait_ge(sm.h, sm.n)
                        ii = s.activation(silu[:, 0:iw], fgb[pb][:, 0:iw], AF.Silu)
                        se.inc(ii)

                    @blk.vector
                    def _(v, sl=sl, ib=ib, iw=iw, pb=pb, gi=gi):
                        v.wait_ge(se.h, se.n)
                        if gi == 0:
                            hqg.wait(v, sl)
                        ii = v.tensor_tensor(htile[:, sl * I + ib:sl * I + ib + iw],
                                             silu[:, 0:iw], fub[pb][:, 0:iw], op=ALU.mult)
                        sv.inc(ii)
                        fgate.release(pb)

                emit_quant(blk, sv, se, htile, sl * I, h_u, h_qg, h_t0, h_sg,
                           h_bm, h_sc, h_rs, h_out, sl * I, I)

                @blk.scalar
                def _(s, tile=tile, sl=sl):
                    s.wait_ge(sv.h, sv.n)
                    s.dma_start(out=hq_dram.ap()[tile * 128:(tile + 1) * 128, :],
                                in_=h_out[:, sl * I:(sl + 1) * I]).then_inc(ss.h, 16)
                ss.n += 16
                hqg.release(sl)
        HQ_STORED = ss.n
        p4.close()

        p5 = ctx.enter_context(contextlib.ExitStack())
        hqfm = p5.enter_context(nc.sbuf_tensor("hqfm", [128, 22 * NQ], BF16))
        wdt = p5.enter_context(nc.sbuf_tensor("wdt", [128, 22 * C], BF16))
        x2tile = p5.enter_context(nc.sbuf_tensor("x2tile", [128, 2 * C], F32))
        v2res = p5.enter_context(nc.sbuf_tensor("v2res", [128, 2 * C], F32))
        otile = p5.enter_context(nc.sbuf_tensor("otile", [128, 2 * C], F32))
        ln2 = p5.enter_context(nc.sbuf_tensor("ln2", [128, 16], F32))

        @blk.sync
        def _(sy):
            sy.wait_ge(ss.h, HQ_STORED)
            for c in range(22):
                sy.dma_start(hqfm[:, c * NQ:(c + 1) * NQ],
                             hq_dram.ap()[0:NQ, c * 128:(c + 1) * 128],
                             transpose=True).then_inc(st.h, 16)
            for c in range(22):
                sy.dma_start(wdt[:, c * C:(c + 1) * C],
                             wdq.ap()[0:C, c * 128:(c + 1) * 128],
                             transpose=True).then_inc(st.h, 16)
        st.n += 44 * 16
        P5_L = st.n

        with (nc.psum_tensor("dn0", [128, 512], F32) as dn0,
              nc.psum_tensor("dn1", [128, 512], F32) as dn1,
              nc.psum_tensor("dn2", [128, 512], F32) as dn2,
              nc.psum_tensor("dn3", [128, 512], F32) as dn3):
            dnb = [dn0, dn1, dn2, dn3]
            dgate = Gate(sv, 4)
            xslot = Gate(sv, 2)
            ygate = Gate(so, 2)

            for tile in range(16):
                sl = tile % 2
                b0, b1 = sl * 2, sl * 2 + 1

                @blk.scalar
                def _(s, tile=tile, sl=sl):
                    xslot.wait(s, sl)
                    s.dma_start(out=x2tile[:, sl * C:(sl + 1) * C],
                                in_=x2_dram.ap()[tile * 128:(tile + 1) * 128, :]).then_inc(ss.h, 16)
                ss.n += 16
                X2T_L = ss.n

                @blk.tensor
                def _(t, tile=tile, b0=b0, b1=b1):
                    if tile == 0:
                        t.wait_ge(st.h, P5_L)
                    dgate.wait(t, b0)
                    dgate.wait(t, b1)
                    for dg in range(2):
                        pb = dnb[b0 if dg == 0 else b1]
                        for cc in range(22):
                            ii = t.matmul(pb[:, :],
                                          hqfm[:, cc * NQ + tile * 128:cc * NQ + (tile + 1) * 128],
                                          wdt[:, cc * C + dg * 512:cc * C + (dg + 1) * 512],
                                          start=(cc == 0), stop=(cc == 21))
                    sm.inc(ii)

                @blk.vector
                def _(v, sl=sl, b0=b0, b1=b1, X2T_L=X2T_L):
                    v.wait_ge(sm.h, sm.n)
                    v.wait_ge(ss.h, X2T_L)
                    ygate.wait(v, sl)
                    v.tensor_tensor(v2res[:, sl * C:sl * C + 512], dnb[b0][:, :],
                                    x2tile[:, sl * C:sl * C + 512], op=ALU.add)
                    ii = v.tensor_tensor(v2res[:, sl * C + 512:(sl + 1) * C], dnb[b1][:, :],
                                         x2tile[:, sl * C + 512:(sl + 1) * C], op=ALU.add)
                    sv.inc(ii)
                    dgate.release(b0)
                    dgate.release(b1)
                    xslot.release(sl)

                emit_ln(blk, sv, se, v2res, otile, ln2, sl, sqdump)

                @blk.scalar
                def _(s, tile=tile, sl=sl):
                    s.wait_ge(sv.h, sv.n)
                    s.dma_start(out=yout.ap()[tile * 128:(tile + 1) * 128, :],
                                in_=otile[:, sl * C:(sl + 1) * C]).then_inc(so.h, 16)
                so.n += 16
                ygate.release(sl)

        @blk.scalar
        def _(s):
            s.wait_ge(so.h, so.n)

    return nc


_CACHE = {}


def kernel(L0_attn_x, L0_attn_W_attn, L0_attn_W_proj, L0_attn_lora_attn_A,
           L0_attn_lora_attn_B, L0_attn_lora_proj_A, L0_attn_lora_proj_B,
           L0_attn_bn_gamma, L0_attn_bn_beta, L0_ffn_w_gate, L0_ffn_w_up,
           L0_ffn_w_down):
    x = np.ascontiguousarray(np.asarray(L0_attn_x, dtype=np.float32))
    wa = np.ascontiguousarray(np.asarray(L0_attn_W_attn, dtype=np.float32))
    wp = np.ascontiguousarray(np.asarray(L0_attn_W_proj, dtype=np.float32))
    A1 = np.ascontiguousarray(np.asarray(L0_attn_lora_attn_A, dtype=np.float32))
    B1 = np.ascontiguousarray(np.asarray(L0_attn_lora_attn_B, dtype=np.float32))
    A2 = np.ascontiguousarray(np.asarray(L0_attn_lora_proj_A, dtype=np.float32))
    B2 = np.ascontiguousarray(np.asarray(L0_attn_lora_proj_B, dtype=np.float32))
    gam = np.asarray(L0_attn_bn_gamma, dtype=np.float32)
    bet = np.asarray(L0_attn_bn_beta, dtype=np.float32)
    wg = np.ascontiguousarray(np.asarray(L0_ffn_w_gate, dtype=np.float32))
    wu = np.ascontiguousarray(np.asarray(L0_ffn_w_up, dtype=np.float32))
    wd = np.ascontiguousarray(np.asarray(L0_ffn_w_down, dtype=np.float32))

    if "nc" not in _CACHE:
        _CACHE["nc"] = build_nc()
    nc = _CACHE["nc"]

    gam2d = np.ascontiguousarray(gam.reshape(24, 128).T)
    bet2d = np.ascontiguousarray(bet.reshape(24, 128).T)
    ones = np.ones((128, 1), dtype=np.float32)
    shsz = 128 * SHF

    in_maps = []
    for c in range(NCORE):
        s, h = c % 4, c // 4
        own = x[s, h * NQ:(h + 1) * NQ, :]
        sib = x[s, (1 - h) * NQ:(2 - h) * NQ, :]
        xp = np.ascontiguousarray(np.concatenate([own, sib], axis=0))
        kidx_row = np.concatenate([np.arange(h * NQ, (h + 1) * NQ),
                                   np.arange((1 - h) * NQ, (2 - h) * NQ)]).astype(np.float32)
        kidxp = np.ascontiguousarray(np.broadcast_to(kidx_row, (128, TOK)))
        qp = (h * NQ + (np.arange(16) * 128)[None, :] + np.arange(128)[:, None]).astype(np.float32)
        in_maps.append({
            "x": xp,
            "w_attn": wa, "w_proj": wp,
            "laa": A1, "lab": B1, "lpa": A2, "lpb": B2,
            "gam2d": gam2d, "bet2d": bet2d,
            "wg_sh": np.ascontiguousarray(wg.reshape(-1)[c * shsz:(c + 1) * shsz].reshape(128, SHF)),
            "wu_sh": np.ascontiguousarray(wu.reshape(-1)[c * shsz:(c + 1) * shsz].reshape(128, SHF)),
            "wd_sh": np.ascontiguousarray(wd.reshape(-1)[c * shsz:(c + 1) * shsz].reshape(128, SHF)),
            "kidx": kidxp, "qpos": np.ascontiguousarray(qp), "ones": ones,
        })

    res = run_bass_kernel_spmd(nc, in_maps, core_ids=list(range(NCORE)))

    out = np.empty((B, T, C), dtype=np.float32)
    for c in range(NCORE):
        s, h = c % 4, c // 4
        out[s, h * NQ:(h + 1) * NQ, :] = res.results[c]["y"]
    return out
